# version 36
# baseline (speedup 1.0000x reference)
import numpy as np
from contextlib import ExitStack

import concourse.bass as bass
import concourse.mybir as mybir
from concourse.bass_utils import run_bass_kernel_spmd

F32 = mybir.dt.float32
BF16 = mybir.dt.bfloat16
AF = mybir.ActivationFunctionType
ALU = mybir.AluOpType
AX = mybir.AxisListType

P = 128
D = 1024
K3 = 3 * D
HC = D // P
NE = 8
NSEQ = 2
TSEQ = 4096
TOK = NSEQ * TSEQ
TS = 256
NTT = TS // P
NST_SEQ = TSEQ // TS
NST = NSEQ * NST_SEQ
EPS = 1e-5
N_CORES = 8


class Em:

    ENGINES = ("pe", "act", "dve", "pool")
    DEBUG = False

    def __init__(self, nc, es):
        self.oplog = []
        self.nc = nc
        self.streams = {"pe": [], "act": [], "dve": [], "pool": [], "sp": []}
        self.cnt = {e: 0 for e in self.ENGINES}
        self.esem = {e: es.enter_context(nc.semaphore("es_" + e))
                     for e in self.ENGINES}
        self.es = es
        self.dsem = {}
        self.dcnt = {}
        self.hw = {s: {} for s in self.streams}
        self.reg = {}
        self.floor = []

    def region(self, key, psum=False):
        self.reg[key] = {"w": [], "r": [], "psum": psum}

    def _get(self, key):
        if key not in self.reg:
            self.region(key)
        return self.reg[key]

    def _collect(self, reads, writes):
        deps = list(self.floor)
        for k in reads:
            R = self._get(k)
            deps += R["w"] + (R["r"] if R["psum"] else [])
        for k in writes:
            R = self._get(k)
            deps += R["w"] + R["r"]
        return deps

    def _waits(self, stream, deps):
        hw = self.hw[stream]
        waits = []
        for tok in deps:
            if tok is None:
                continue
            if tok[0] == "e":
                _, teng, n = tok
                if teng == stream and stream == "pe":
                    continue
                key = "e:" + teng
                if hw.get(key, 0) >= n:
                    continue
                hw[key] = n
                waits.append((self.esem[teng], n))
            else:
                _, nm, v = tok
                key = "d:" + nm
                if hw.get(key, 0) >= v:
                    continue
                hw[key] = v
                waits.append((self.dsem[nm], v))
        return waits

    def _update(self, reads, writes, tok):
        for k in reads:
            R = self._get(k)
            if R["psum"]:
                R["w"] = [tok]
                R["r"] = []
            else:
                R["r"].append(tok)
        for k in writes:
            R = self._get(k)
            R["w"] = [tok]
            R["r"] = []

    def op(self, eng, emit, reads=(), writes=(), track=True):
        waits = self._waits(eng, self._collect(reads, writes))
        if self.DEBUG:
            import sys as _s
            self.oplog.append((eng, self.cnt[eng] + (1 if track else 0),
                               _s._getframe(1).f_lineno, reads, writes,
                               [(w[0].name, w[1]) for w in waits], track))
        sem = self.esem[eng]
        if track:
            self.cnt[eng] += 1

            def run(e, waits=waits, emit=emit, sem=sem):
                for s, v in waits:
                    e.wait_ge(s, v)
                emit(e).then_inc(sem, 1)
        else:
            def run(e, waits=waits, emit=emit):
                for s, v in waits:
                    e.wait_ge(s, v)
                emit(e)

        self.streams[eng].append(run)
        if not track:
            return None
        tok = ("e", eng, self.cnt[eng])
        self._update(reads, writes, tok)
        return tok

    def dma(self, queue, semname, emit, reads=(), writes=()):
        if semname not in self.dsem:
            self.dsem[semname] = self.es.enter_context(
                self.nc.semaphore("ds_" + semname))
            self.dcnt[semname] = 0
        waits = self._waits(queue, self._collect(reads, writes))
        if self.DEBUG:
            import sys as _s
            self.oplog.append((queue + ":" + semname, self.dcnt[semname] + 1,
                               _s._getframe(1).f_lineno, reads, writes,
                               [(w[0].name, w[1]) for w in waits], True))
        self.dcnt[semname] += 1
        target = 16 * self.dcnt[semname]
        sem = self.dsem[semname]

        def run(e, waits=waits, emit=emit, sem=sem):
            for s, v in waits:
                e.wait_ge(s, v)
            emit(e).then_inc(sem, 16)

        self.streams[queue].append(run)
        tok = ("d", semname, target)
        self._update(reads, writes, tok)
        return tok

    def set_floor(self):
        self.floor = ([("e", e, self.cnt[e]) for e in self.ENGINES] +
                      [("d", nm, 16 * self.dcnt[nm]) for nm in self.dcnt])

    def finalize_sp(self):
        deps = [("d", nm, 16 * self.dcnt[nm]) for nm in self.dcnt]
        waits = self._waits("sp", deps)

        def run(e, waits=waits):
            for s, v in waits:
                e.wait_ge(s, v)

        self.streams["sp"].append(run)


def build_nc(phases=("a1", "a2", "b")):
    nc = bass.Bass("TRN2", target_bir_lowering=False, debug=False)

    x_h = nc.dram_tensor("x", [TOK, D], F32, kind="ExternalInput").ap()
    w0_h = nc.dram_tensor("w0", [D, K3], F32, kind="ExternalInput").ap()
    w1_h = nc.dram_tensor("w1", [D, K3], F32, kind="ExternalInput").ap()
    rw_h = nc.dram_tensor("rw", [P, HC * NE], F32,
                          kind="ExternalInput").ap()
    rb_h = nc.dram_tensor("rb", [1, NE], F32, kind="ExternalInput").ap()
    ew_h = nc.dram_tensor("ew", [NE * D, D], F32, kind="ExternalInput").ap()
    eb_h = nc.dram_tensor("eb", [NE, D], F32, kind="ExternalInput").ap()
    id_h = nc.dram_tensor("ident", [P, P], F32, kind="ExternalInput").ap()

    out_h = nc.dram_tensor("out", [TOK, D], F32, kind="ExternalOutput").ap()

    xnthi_h = nc.dram_tensor("xnthi", [NST, P, NTT * D], BF16,
                             kind="Internal").ap()
    h1t_h = nc.dram_tensor("h1t", [NST, P, NTT * D], F32, kind="Internal").ap()
    s_h = nc.dram_tensor("s", [TOK, D], F32, kind="Internal").ap()

    def tokslice(hbm, tok0):
        return hbm[tok0:tok0 + TS, :].rearrange("(a p) h -> p a h", p=P)

    with ExitStack() as es:
        em = Em(nc, es)

        def sbuf(name, shape, dt=F32, stack=None, side=None):
            t = (stack or es).enter_context(
                nc.sbuf_tensor(name, shape, dt, side=side))
            em.region(name)
            return t

        def psum(name, shape, dt=F32):
            t = es.enter_context(nc.psum_tensor(name, shape, dt))
            em.region(name, psum=True)
            return t

        zb = [psum("z%d" % i, [P, 512]) for i in range(6)]
        trb = [psum("tr%d" % i, [P, 512]) for i in range(2)]
        psr = trb[0]

        id_sb = sbuf("idsb", [P, P], F32, side="right")
        rwf = sbuf("rwf", [P, HC * NE], F32, side="right")
        wrh = sbuf("wrh", [P, HC * NE], BF16, side="right")
        wrl = sbuf("wrl", [P, HC * NE], BF16, side="right")
        rbf = sbuf("rbf", [1, NE], F32, side="right")
        rbh = sbuf("rbh", [1, NE], BF16, side="right")
        rbl = sbuf("rbl", [1, NE], BF16, side="right")
        ones_r = sbuf("ones_r", [1, TS], BF16, side="right")
        eb_sb = sbuf("ebsb", [NE, D], F32, side="right")
        ebh_sb = sbuf("ebh", [NE, D], BF16, side="right")
        onehot = sbuf("onehot", [P, (TOK // P) * NE], F32, side="right")
        gate = sbuf("gate", [P, TOK // P], F32, side="right")
        carry = sbuf("carry", [P, 2 * NSEQ * HC], F32, side="right")
        stat = sbuf("stat", [P, 96], F32, side="right")
        logt = sbuf("logt", [NE, TS], F32, side="right")
        logits = sbuf("logits", [P, NTT * NE], F32, side="right")
        ohT = sbuf("ohT", [NE, P], BF16, side="right")
        sq8 = sbuf("sq8", [P, 2 * NE], F32, side="right")

        em.dma("sp", "cid", lambda e: e.dma_start(out=id_sb[:], in_=id_h),
               writes=("idsb",))
        em.dma("sp", "crw", lambda e: e.dma_start(out=rwf[:], in_=rw_h),
               writes=("rwf",))
        em.dma("sp", "crb", lambda e: e.dma_start(out=rbf[:], in_=rb_h),
               writes=("rbf",))
        em.dma("sp", "ceb", lambda e: e.dma_start(out=eb_sb[:], in_=eb_h),
               writes=("ebsb",))
        em.op("dve", lambda e: e.tensor_copy(out=wrh[:], in_=rwf[:]),
              reads=("rwf",), writes=("wrh",))
        em.op("dve", lambda e: e.tensor_sub(wrl[:], rwf[:], wrh[:]),
              reads=("rwf", "wrh"), writes=("wrl",))
        em.op("dve", lambda e: e.tensor_copy(out=rbh[:], in_=rbf[:]),
              reads=("rbf",), writes=("rbh",))
        em.op("dve", lambda e: e.tensor_sub(rbl[:], rbf[:], rbh[:]),
              reads=("rbf", "rbh"), writes=("rbl",))
        em.op("pool", lambda e: e.memset(ones_r[:], 1.0), writes=("ones_r",))
        em.op("dve", lambda e: e.tensor_copy(out=ebh_sb[:], in_=eb_sb[:]),
              reads=("ebsb",), writes=("ebh",))

        es_w0 = ExitStack()
        w0_sb = sbuf("w0sb", [P, HC * K3], BF16, stack=es_w0)
        es_w1 = ExitStack()
        w1_sb = sbuf("w1sb", [P, HC * K3], BF16, stack=es_w1, side="right")
        for c in range(4):
            for (wsb, wh, nm) in ((w0_sb, w0_h, "w0"), (w1_sb, w1_h, "w1")):
                em.dma("pool", "%s%d" % (nm, c),
                       lambda e, wsb=wsb, wh=wh, c=c: e.dma_start(
                           out=wsb[:, c * 2 * K3:(c + 1) * 2 * K3],
                           in_=wh[c * 256:(c + 1) * 256, :].rearrange(
                               "(a p) k -> p a k", p=P)),
                       writes=("%s.%d" % (nm, c),))

        es_a = ExitStack()
        in0 = [sbuf("in0%d" % i, [P, NTT * D], stack=es_a) for i in range(2)]
        t32 = [sbuf("t32%d" % i, [P, NTT * D], stack=es_a) for i in range(2)]
        thi = [sbuf("thi%d" % i, [P, NTT * D], BF16, stack=es_a)
               for i in range(2)]
        tlo = sbuf("tlo", [P, NTT * D], BF16, stack=es_a)
        Cb = sbuf("Cb", [P, NTT * D], stack=es_a)
        SG = sbuf("SGb", [P, NTT * D], stack=es_a)
        HP = sbuf("HPb", [P, NTT * D], stack=es_a)
        SPb = sbuf("SPb", [P, NTT * D], stack=es_a)
        OUT = sbuf("OUTb", [P, NTT * D], stack=es_a)
        CMB = sbuf("CMBb", [P, NTT * D], stack=es_a)
        SQ = sbuf("SQb", [P, D], stack=es_a)

        iters = [(s, k) for k in range(NST_SEQ) for s in range(NSEQ)]

        def ln_part(it, xkey, xb):
            base = (it % 2) * 40
            SK = "statln%d" % (it % 2)
            st_ = stat[:, base:base + 40]
            for tt in range(NTT):
                for hh in range(2):
                    em.op("dve", lambda e, tt=tt, hh=hh: e.bn_stats(
                        out=st_[:, tt * 12 + hh * 6:tt * 12 + hh * 6 + 6],
                        in_=xb[:, tt * D + hh * 512:tt * D + (hh + 1) * 512]),
                        reads=(xkey,), writes=(SK,))
                em.op("dve", lambda e, tt=tt: e.bn_aggr(
                    out=st_[:, 24 + 2 * tt:26 + 2 * tt],
                    in_=st_[:, tt * 12:tt * 12 + 12]),
                    reads=(SK,), writes=(SK,))
            vview = st_[:, 24:28].rearrange("p (a b) -> p b a", b=2)[:, 1]
            em.op("dve", lambda e, vview=vview: e.tensor_scalar(
                out=st_[:, 28:30], in0=vview, scalar1=EPS,
                scalar2=None, op0=ALU.add), reads=(SK,), writes=(SK,))
            em.op("dve", lambda e: e.tensor_scalar(
                out=st_[:, 30:32], in0=st_[:, 28:30], scalar1=-0.5,
                scalar2=1.5, op0=ALU.mult, op1=ALU.add),
                reads=(SK,), writes=(SK,))
            for _ in range(3):
                em.op("dve", lambda e: e.tensor_mul(
                    st_[:, 32:34], st_[:, 30:32], st_[:, 30:32]),
                    reads=(SK,), writes=(SK,))
                em.op("dve", lambda e: e.tensor_mul(
                    st_[:, 32:34], st_[:, 32:34], st_[:, 28:30]),
                    reads=(SK,), writes=(SK,))
                em.op("dve", lambda e: e.tensor_scalar(
                    out=st_[:, 32:34], in0=st_[:, 32:34], scalar1=-0.5,
                    scalar2=1.5, op0=ALU.mult, op1=ALU.add),
                    reads=(SK,), writes=(SK,))
                em.op("dve", lambda e: e.tensor_mul(
                    st_[:, 30:32], st_[:, 30:32], st_[:, 32:34]),
                    reads=(SK,), writes=(SK,))
            for tt in range(NTT):
                xs = xb[:, tt * D:(tt + 1) * D]
                em.op("dve", lambda e, xs=xs, tt=tt: e.tensor_scalar(
                    out=xs, in0=xs, scalar1=st_[:, 24 + 2 * tt:25 + 2 * tt],
                    scalar2=st_[:, 30 + tt:31 + tt],
                    op0=ALU.subtract, op1=ALU.mult),
                    reads=(xkey, SK), writes=(xkey,))

        def tr_part(it, xkey, xb, tkey, dst32):
            blocks = [(tt, hc) for tt in range(NTT) for hc in range(HC)]
            d3 = dst32[:].rearrange("p (h t) -> p h t", h=HC)
            for bi in range(4):
                bank = trb[bi % 2]
                bkey = "tr%d" % (bi % 2)
                tt0, hc0 = blocks[bi * 4]
                for j in range(4):
                    tt, hc = blocks[bi * 4 + j]
                    em.op("pe", lambda e, tt=tt, hc=hc, bank=bank, j=j:
                          e.transpose(out=bank[:, j * P:(j + 1) * P],
                                      in_=xb[:, tt * D + hc * P:
                                             tt * D + (hc + 1) * P],
                                      identity=id_sb[:]),
                          reads=(xkey, "idsb"), writes=(bkey,),
                          track=(j == 3))
                em.op("act", lambda e, bank=bank, tt0=tt0, hc0=hc0:
                      e.copy(out=d3[:, hc0:hc0 + 4, tt0 * P:(tt0 + 1) * P],
                             in_=bank[:]),
                      reads=(bkey,), writes=(tkey,))

        def router(it, s, k):
            cur = it % 2
            th, tl = thi[cur], tlo
            thk = "thi%d" % cur
            em.op("pe", lambda e: e.matmul(
                out=psr[0:NE, 0:TS], lhsT=rbh[:], rhs=ones_r[:],
                start=True, stop=False),
                reads=("rbh", "ones_r"), writes=("tr0",), track=False)
            em.op("pe", lambda e: e.matmul(
                out=psr[0:NE, 0:TS], lhsT=rbl[:], rhs=ones_r[:],
                start=False, stop=False),
                reads=("rbl", "ones_r"), writes=("tr0",), track=False)
            n = 0
            for hc in range(HC):
                for (a, b, ak, bk) in ((wrh, th, "wrh", thk),
                                       (wrh, tl, "wrh", "tlo"),
                                       (wrl, th, "wrl", thk)):
                    n += 1
                    last = (n == 24)
                    em.op("pe", lambda e, a=a, b=b, hc=hc, last=last:
                          e.matmul(out=psr[0:NE, 0:TS],
                                   lhsT=a[:, hc * NE:(hc + 1) * NE],
                                   rhs=b[:, hc * TS:(hc + 1) * TS],
                                   start=False, stop=last),
                          reads=(ak, bk), writes=("tr0",), track=last)
            em.op("act", lambda e: e.copy(out=logt[:], in_=psr[0:NE, 0:TS]),
                  reads=("tr0",), writes=("logt",))
            for tt in range(NTT):
                em.op("pe", lambda e, tt=tt: e.transpose(
                    out=zb[5][:, tt * NE:(tt + 1) * NE],
                    in_=logt[:, tt * P:(tt + 1) * P],
                    identity=id_sb[0:NE, 0:NE]),
                    reads=("logt", "idsb"), writes=("z5",))
            em.op("dve", lambda e: e.tensor_copy(
                out=logits[:, 0:NTT * NE], in_=zb[5][:, 0:NTT * NE]),
                reads=("z5",), writes=("logits",))
            for tt in range(NTT):
                j = (s * TSEQ + k * TS) // P + tt
                lg = logits[:, tt * NE:(tt + 1) * NE]
                c0 = 24 + 2 * tt
                em.op("dve", lambda e, lg=lg, c0=c0: e.tensor_reduce(
                    out=stat[:, c0:c0 + 1], in_=lg, axis=AX.X,
                    op=ALU.max, negate=True),
                    reads=("logits",), writes=("statr",))
                em.op("dve", lambda e, lg=lg, c0=c0, j=j: e.tensor_scalar(
                    out=onehot[:, j * NE:(j + 1) * NE], in0=lg,
                    scalar1=stat[:, c0:c0 + 1], scalar2=0.0,
                    op0=ALU.add, op1=ALU.is_equal),
                    reads=("logits", "statr"), writes=("onehot",))
                em.op("act", lambda e, lg=lg, c0=c0: e.activation(
                    out=sq8[:, 0:NE], in_=lg, func=AF.Sigmoid,
                    bias=stat[:, c0:c0 + 1]),
                    reads=("logits", "statr"), writes=("sq8",))
                em.op("dve", lambda e: e.tensor_scalar(
                    out=sq8[:, NE:2 * NE], in0=sq8[:, 0:NE], scalar1=-1.0,
                    scalar2=1.0, op0=ALU.mult, op1=ALU.add),
                    reads=("sq8",), writes=("sq8",))
                em.op("dve", lambda e: e.reciprocal(
                    out=sq8[:, NE:2 * NE], in_=sq8[:, NE:2 * NE]),
                    reads=("sq8",), writes=("sq8",))
                em.op("dve", lambda e: e.tensor_mul(
                    sq8[:, NE:2 * NE], sq8[:, 0:NE], sq8[:, NE:2 * NE]),
                    reads=("sq8",), writes=("sq8",))
                em.op("dve", lambda e, c0=c0: e.reduce_sum(
                    out=stat[:, c0 + 1:c0 + 2], in_=sq8[:, NE:2 * NE],
                    axis=AX.X), reads=("sq8",), writes=("statr",))
                em.op("dve", lambda e, c0=c0, j=j: e.reciprocal(
                    out=gate[:, j:j + 1], in_=stat[:, c0 + 1:c0 + 2]),
                    reads=("statr",), writes=("gate",))

        def z_part(wsb, wname, rkey, rhs):
            for r in range(24):
                bank = zb[r % 6]
                bkey = "z%d" % (r % 6)
                sl = (r // 6) % 2
                outap = bank[:, sl * TS:(sl + 1) * TS]
                for hc in range(HC):
                    em.op("pe", lambda e, r=r, hc=hc, outap=outap: e.matmul(
                        out=outap,
                        lhsT=wsb[:, hc * K3 + r * P: hc * K3 + (r + 1) * P],
                        rhs=rhs[:, hc * TS:(hc + 1) * TS],
                        start=(hc == 0), stop=(hc == HC - 1)),
                        reads=("%s.%d" % (wname, hc // 2), rkey),
                        writes=(bkey,), track=(hc == HC - 1))
                if r < 8:
                    fg = r
                    em.op("act", lambda e, outap=outap, fg=fg: e.activation(
                        out=SG[:, fg * TS:(fg + 1) * TS], in_=outap,
                        func=AF.Sigmoid), reads=(bkey,), writes=("SGb",))
                    em.op("act", lambda e, outap=outap, fg=fg: e.activation(
                        out=HP[:, fg * TS:(fg + 1) * TS], in_=outap,
                        func=AF.Copy, bias=0.5),
                        reads=(bkey,), writes=("HPb",))
                elif r < 16:
                    fg = r - 8
                    em.op("act", lambda e, outap=outap, fg=fg: e.activation(
                        out=Cb[:, fg * TS:(fg + 1) * TS], in_=outap,
                        func=AF.Sigmoid, scale=-1.0),
                        reads=(bkey,), writes=("Cb",))
                else:
                    fg = r - 16
                    em.op("act", lambda e, outap=outap, fg=fg: e.activation(
                        out=SPb[:, fg * TS:(fg + 1) * TS], in_=outap,
                        func=AF.Sigmoid), reads=(bkey,), writes=("SPb",))
        def post_part(s, k, layer, skey, src32):
            em.op("dve", lambda e: e.tensor_max(HP[:], SG[:], HP[:]),
                  reads=("SGb", "HPb"), writes=("HPb",))
            em.op("dve", lambda e: e.scalar_tensor_tensor(
                out=HP[:], in0=Cb[:], scalar=1.0, in1=HP[:],
                op0=ALU.subtract, op1=ALU.mult),
                reads=("Cb", "HPb"), writes=("HPb",))
            cbase = layer * NSEQ * HC + s * HC
            for fg in range(HC):
                if k == 0:
                    init = 0.0
                    creads = ("Cb", "HPb")
                else:
                    init = carry[:, cbase + fg:cbase + fg + 1]
                    creads = ("Cb", "HPb", "carry")
                em.op("dve", lambda e, fg=fg, init=init: e.tensor_tensor_scan(
                    out=OUT[:, fg * TS:(fg + 1) * TS],
                    data0=Cb[:, fg * TS:(fg + 1) * TS],
                    data1=HP[:, fg * TS:(fg + 1) * TS],
                    initial=init, op0=ALU.mult, op1=ALU.add),
                    reads=creads, writes=("OUTb",))
            em.op("dve", lambda e, cbase=cbase: e.tensor_copy(
                out=carry[:, cbase:cbase + HC],
                in_=OUT[:].rearrange("p (h t) -> p h t", h=HC)[:, :,
                                                             TS - 1:TS]),
                reads=("OUTb",), writes=("carry",))
            em.op("dve", lambda e: e.tensor_add(OUT[:], OUT[:], src32[:]),
                  reads=("OUTb", skey), writes=("OUTb",))
            em.op("dve", lambda e: e.tensor_mul(OUT[:], SPb[:], OUT[:]),
                  reads=("SPb", "OUTb"), writes=("OUTb",))
            em.op("dve", lambda e: e.tensor_sub(CMB[:], src32[:], OUT[:]),
                  reads=(skey, "OUTb"), writes=("CMBb",))

        if "a1" in phases:
            NIT = len(iters)

            def a1_load(it):
                s2, k2 = iters[it]
                em.dma("sp", "in0%s" % "ab"[it % 2],
                       lambda e, s2=s2, k2=k2, it=it: e.dma_start(
                           out=in0[it % 2][:].rearrange(
                               "p (a h) -> p a h", a=NTT),
                           in_=tokslice(x_h, s2 * TSEQ + k2 * TS)),
                       writes=("in0%d" % (it % 2),))

            def a1_cast(it):
                cur = it % 2
                st = iters[it][0] * NST_SEQ + iters[it][1]
                em.op("act", lambda e, cur=cur: e.copy(
                    out=thi[cur][:], in_=t32[cur][:]),
                    reads=("t32%d" % cur,), writes=("thi%d" % cur,))
                em.op("dve", lambda e, cur=cur: e.tensor_sub(
                    tlo[:], t32[cur][:], thi[cur][:]),
                    reads=("t32%d" % cur, "thi%d" % cur), writes=("tlo",))
                em.dma("pool", "sthi%s" % "ab"[cur],
                       lambda e, cur=cur, st=st: e.dma_start(
                           out=xnthi_h[st], in_=thi[cur][:]),
                       reads=("thi%d" % cur,), writes=("hbm.xnthi.%d" % st,))

            for it in range(-2, NIT):
                if 0 <= it + 2 < NIT:
                    a1_load(it + 2)
                    ln_part(it + 2, "in0%d" % ((it + 2) % 2), in0[(it + 2) % 2])
                if 0 <= it + 1 < NIT:
                    c1 = (it + 1) % 2
                    tr_part(it + 1, "in0%d" % c1, in0[c1], "t32%d" % c1,
                            t32[c1])
                    a1_cast(it + 1)
                    s2, k2 = iters[it + 1]
                    router(it + 1, s2, k2)
                if 0 <= it:
                    cur = it % 2
                    z_part(w0_sb, "w0", "thi%d" % cur, thi[cur])
                    s1, k1 = iters[it]
                    st = s1 * NST_SEQ + k1
                    post_part(s1, k1, 0, "t32%d" % (it % 2), t32[it % 2])
                    em.dma("pool", "scmb", lambda e, st=st: e.dma_start(
                        out=h1t_h[st], in_=CMB[:]),
                        reads=("CMBb",), writes=("hbm.h1t.%d" % st,))

        if "a2" in phases:
            NIT = len(iters)

            def a2_load(it):
                s2, k2 = iters[it]
                st2 = s2 * NST_SEQ + k2
                em.dma("sp", "in0%s" % "ab"[it % 2],
                       lambda e, st2=st2, it=it: e.dma_start(
                           out=in0[it % 2][:], in_=h1t_h[st2]),
                       reads=("hbm.h1t.%d" % st2,),
                       writes=("in0%d" % (it % 2),))

            def a2_trback(it):
                s1, k1 = iters[it]
                stok, stokkey = t32[0], "t320"
                s3 = stok[:].rearrange("p (a h) -> p a h", a=NTT)
                blocks = [(tt, hc) for tt in range(NTT) for hc in range(HC)]
                for bi in range(4):
                    bank = trb[bi % 2]
                    bkey = "tr%d" % (bi % 2)
                    tt0, hc0 = blocks[bi * 4]
                    for j in range(4):
                        tt, hc = blocks[bi * 4 + j]
                        em.op("pe", lambda e, tt=tt, hc=hc, bank=bank, j=j:
                              e.transpose(
                                  out=bank[:, j * P:(j + 1) * P],
                                  in_=CMB[:, hc * TS + tt * P:
                                          hc * TS + (tt + 1) * P],
                                  identity=id_sb[:]),
                              reads=("CMBb", "idsb"), writes=(bkey,),
                              track=(j == 3))
                    em.op("act", lambda e, bank=bank, tt0=tt0, hc0=hc0:
                          e.copy(out=s3[:, tt0, hc0 * P:(hc0 + 4) * P],
                                 in_=bank[:]),
                          reads=(bkey,), writes=(stokkey,))
                em.dma("pool", "sst", lambda e, s1=s1, k1=k1: e.dma_start(
                    out=tokslice(s_h, s1 * TSEQ + k1 * TS), in_=s3),
                    reads=(stokkey,),
                    writes=("hbm.s.%d" % (s1 * TSEQ + k1 * TS),))

            for it in range(0, NIT + 1):
                if it < NIT:
                    if it == 0:
                        a2_load(0)
                        em.op("dve", lambda e: e.tensor_copy(
                            out=thi[0][:], in_=in0[0][:]),
                            reads=("in00",), writes=("thi0",))
                    if it + 1 < NIT:
                        a2_load(it + 1)
                        em.op("dve", lambda e, nxt=(it + 1) % 2:
                              e.tensor_copy(out=thi[nxt][:], in_=in0[nxt][:]),
                              reads=("in0%d" % ((it + 1) % 2),),
                              writes=("thi%d" % ((it + 1) % 2),))
                    cur = it % 2
                    z_part(w1_sb, "w1", "thi%d" % cur, thi[cur])
                if it - 1 >= 0:
                    a2_trback(it - 1)
                if it < NIT:
                    s1, k1 = iters[it]
                    post_part(s1, k1, 1, "in0%d" % (it % 2), in0[it % 2])

        if "b" in phases:
            es_a.close()
            es_w0.close()
            es_w1.close()
            em.set_floor()
            exp = [sbuf("exp%d" % e2, [P, HC * D], BF16) for e2 in range(NE)]
            bx = [sbuf("bx%d" % i, [P, NTT * D]) for i in range(2)]
            bs = [sbuf("bs%d" % i, [P, NTT * D]) for i in range(2)]
            bh = [sbuf("bh%d" % i, [P, NTT * D], BF16) for i in range(2)]
            bo = [sbuf("bo%d" % i, [P, NTT * D]) for i in range(2)]
            bacc = [sbuf("bacc%d" % i, [P, D]) for i in range(2)]
            bg = sbuf("bg", [P, D])
            for e2 in range(NE):
                em.dma("pool", "exp%d" % e2, lambda e, e2=e2: e.dma_start(
                    out=exp[e2][:],
                    in_=ew_h[e2 * D:(e2 + 1) * D, :].rearrange(
                        "(c p) d -> p c d", p=P)),
                    writes=("exp%d" % e2,))

            def bload(bst, slot):
                tok0 = bst * TS
                s = tok0 // TSEQ
                k = (tok0 % TSEQ) // TS
                stA = s * NST_SEQ + k
                em.dma("sp", "bx%s" % "ab"[slot],
                       lambda e, tok0=tok0, slot=slot: e.dma_start(
                           out=bx[slot][:].rearrange(
                               "p (a h) -> p a h", a=NTT),
                           in_=tokslice(x_h, tok0)),
                       writes=("bx%d" % slot,))
                em.dma("sp", "bs%s" % "ab"[slot],
                       lambda e, tok0=tok0, slot=slot: e.dma_start(
                           out=bs[slot][:].rearrange(
                               "p (a h) -> p a h", a=NTT),
                           in_=tokslice(s_h, tok0)),
                       reads=("hbm.s.%d" % tok0,),
                       writes=("bs%d" % slot,))
                em.dma("sp", "bh%s" % "ab"[slot],
                       lambda e, stA=stA, slot=slot: e.dma_start(
                           out=bh[slot][:], in_=xnthi_h[stA]),
                       reads=("hbm.xnthi.%d" % stA,),
                       writes=("bh%d" % slot,))

            bload(0, 0)
            pairbanks = [[0, 1, 2, 3], [4, 5, 6, 7]]
            allb = zb + trb
            bkeys = ["z%d" % i for i in range(6)] + ["tr0", "tr1"]

            def bias_part(j):
                bslot = j % 2
                bk = "bacc%d" % bslot
                em.op("pe", lambda e, j=j: e.transpose(
                    out=trb[0][0:NE, 0:P],
                    in_=onehot[:, j * NE:(j + 1) * NE],
                    identity=id_sb[:]),
                    reads=("onehot", "idsb"), writes=("tr0",))
                em.op("act", lambda e: e.copy(
                    out=ohT[:], in_=trb[0][0:NE, 0:P]),
                    reads=("tr0",), writes=("ohT",))
                for g in range(2):
                    em.op("pe", lambda e, g=g: e.matmul(
                        out=trb[g][:], lhsT=ohT[:],
                        rhs=ebh_sb[:, g * 512:(g + 1) * 512],
                        start=True, stop=True),
                        reads=("ohT", "ebh"), writes=("tr%d" % g,))
                for g in range(2):
                    em.op("dve", lambda e, g=g, bslot=bslot: e.tensor_copy(
                        out=bacc[bslot][:, g * 512:(g + 1) * 512],
                        in_=trb[g][:]),
                        reads=("tr%d" % g,), writes=(bk,))

            bias_part(0)
            for bst in range(NST):
                cur = bst % 2
                if bst + 1 < NST:
                    bload(bst + 1, (bst + 1) % 2)
                hkey = "bh%d" % cur
                for tt in range(NTT):
                    j = bst * NTT + tt
                    cj = j % 2
                    bk = "bacc%d" % cj
                    for ep in range(NE // 2):
                        grp = pairbanks[ep % 2]
                        if ep == 3 and j + 1 < NST * NTT:
                            bias_part(j + 1)
                        for ei in range(2):
                            e2 = ep * 2 + ei
                            for g in range(2):
                                bi = grp[ei * 2 + g]
                                for hc in range(HC):
                                    em.op("pe", lambda e, e2=e2, hc=hc, g=g,
                                          bank=allb[bi], tt=tt, cur=cur:
                                          e.matmul(
                                              out=bank[:, 0:512],
                                              lhsT=bh[cur][
                                                  :, hc * TS + tt * P:
                                                  hc * TS + (tt + 1) * P],
                                              rhs=exp[e2][
                                                  :, hc * D + g * 512:
                                                  hc * D + (g + 1) * 512],
                                              start=(hc == 0),
                                              stop=(hc == HC - 1)),
                                          reads=(hkey, "exp%d" % e2),
                                          writes=(bkeys[bi],),
                                          track=(hc == HC - 1))
                        for ei in range(2):
                            e2 = ep * 2 + ei
                            oh_col = onehot[:, j * NE + e2:j * NE + e2 + 1]
                            for g in range(2):
                                bi = grp[ei * 2 + g]
                                em.op("dve", lambda e, bank=allb[bi], g=g,
                                      oh_col=oh_col, cj=cj:
                                      e.scalar_tensor_tensor(
                                          out=bacc[cj][:,
                                                       g * 512:(g + 1) * 512],
                                          in0=bank[:, 0:512],
                                          scalar=oh_col,
                                          in1=bacc[cj][:,
                                                       g * 512:(g + 1) * 512],
                                          op0=ALU.mult, op1=ALU.add),
                                      reads=(bkeys[bi], "onehot", bk),
                                      writes=(bk,))
                    em.op("dve", lambda e, j=j, cj=cj: e.tensor_scalar(
                        out=bacc[cj][:], in0=bacc[cj][:],
                        scalar1=gate[:, j:j + 1],
                        scalar2=None, op0=ALU.mult),
                        reads=(bk, "gate"), writes=(bk,))
                    em.op("act", lambda e, cj=cj: e.activation(
                        out=bg[:], in_=bacc[cj][:], func=AF.Silu),
                        reads=(bk,), writes=("bg",))
                    em.op("dve", lambda e, tt=tt, cur=cur: e.tensor_mul(
                        bg[:], bs[cur][:, tt * D:(tt + 1) * D], bg[:]),
                        reads=("bs%d" % cur, "bg"), writes=("bg",))
                    em.op("pool", lambda e, tt=tt, cur=cur: e.tensor_add(
                        bo[cur][:, tt * D:(tt + 1) * D],
                        bx[cur][:, tt * D:(tt + 1) * D], bg[:]),
                        reads=("bx%d" % cur, "bg"), writes=("bo%d" % cur,))
                em.dma("pool", "bo%s" % "ab"[cur],
                       lambda e, bst=bst, cur=cur: e.dma_start(
                           out=tokslice(out_h, bst * TS),
                           in_=bo[cur][:].rearrange(
                               "p (a h) -> p a h", a=NTT)),
                       reads=("bo%d" % cur,))

        em.finalize_sp()

        block = es.enter_context(nc.Block())

        @block.sync
        def _(e):
            for f in em.streams["sp"]:
                f(e)

        @block.tensor
        def _(e):
            for f in em.streams["pe"]:
                f(e)

        @block.scalar
        def _(e):
            for f in em.streams["act"]:
                f(e)

        @block.vector
        def _(e):
            for f in em.streams["dve"]:
                f(e)

        @block.gpsimd
        def _(e):
            for f in em.streams["pool"]:
                f(e)

    nc._em_oplog = em.oplog
    return nc


def kernel(L0_attn_h, L0_attn_W0, L0_attn_W1, L0_ffn_router_w,
           L0_ffn_router_b, L0_ffn_expert_w, L0_ffn_expert_b):
    x = np.ascontiguousarray(np.asarray(L0_attn_h, np.float32)
                             ).reshape(16 * TSEQ, D)
    w0 = np.ascontiguousarray(np.asarray(L0_attn_W0, np.float32))
    w1 = np.ascontiguousarray(np.asarray(L0_attn_W1, np.float32))
    rw = np.ascontiguousarray(
        np.asarray(L0_ffn_router_w, np.float32).reshape(HC, P, NE)
        .transpose(1, 0, 2).reshape(P, HC * NE))
    rb = np.ascontiguousarray(np.asarray(L0_ffn_router_b, np.float32)
                              ).reshape(1, NE)
    ew = np.ascontiguousarray(np.asarray(L0_ffn_expert_w, np.float32)
                              ).reshape(NE * D, D)
    eb = np.ascontiguousarray(np.asarray(L0_ffn_expert_b, np.float32))
    ident = np.eye(P, dtype=np.float32)

    nc = build_nc()
    in_maps = []
    for c in range(N_CORES):
        in_maps.append(dict(x=x[c * TOK:(c + 1) * TOK], w0=w0, w1=w1,
                            rw=rw, rb=rb, ew=ew, eb=eb, ident=ident))
    res = run_bass_kernel_spmd(nc, in_maps, core_ids=list(range(N_CORES)))
    out = np.concatenate([res.results[c]["out"] for c in range(N_CORES)],
                         axis=0)
    return out.reshape(16, TSEQ, D)


# version 37
# speedup vs baseline: 1.2038x; 1.2038x over previous
import numpy as np
from contextlib import ExitStack

import concourse.bass as bass
import concourse.mybir as mybir
from concourse.bass_utils import run_bass_kernel_spmd

F32 = mybir.dt.float32
BF16 = mybir.dt.bfloat16
AF = mybir.ActivationFunctionType
ALU = mybir.AluOpType
AX = mybir.AxisListType

P = 128
D = 1024
K3 = 3 * D
HC = D // P
NE = 8
NSEQ = 2
TSEQ = 4096
TOK = NSEQ * TSEQ
TS = 256
NTT = TS // P
NST_SEQ = TSEQ // TS
NST = NSEQ * NST_SEQ
EPS = 1e-5
N_CORES = 8


class Em:

    ENGINES = ("pe", "act", "dve", "pool")
    DEBUG = False

    def __init__(self, nc, es):
        self.oplog = []
        self.nc = nc
        self.streams = {"pe": [], "act": [], "dve": [], "pool": [], "sp": []}
        self.cnt = {e: 0 for e in self.ENGINES}
        self.esem = {e: es.enter_context(nc.semaphore("es_" + e))
                     for e in self.ENGINES}
        self.es = es
        self.dsem = {}
        self.dcnt = {}
        self.hw = {s: {} for s in self.streams}
        self.reg = {}
        self.floor = []

    def region(self, key, psum=False):
        self.reg[key] = {"w": [], "r": [], "psum": psum}

    def _get(self, key):
        if key not in self.reg:
            self.region(key)
        return self.reg[key]

    def _collect(self, reads, writes):
        deps = list(self.floor)
        for k in reads:
            R = self._get(k)
            deps += R["w"] + (R["r"] if R["psum"] else [])
        for k in writes:
            R = self._get(k)
            deps += R["w"] + R["r"]
        return deps

    def _waits(self, stream, deps):
        hw = self.hw[stream]
        waits = []
        for tok in deps:
            if tok is None:
                continue
            if tok[0] == "e":
                _, teng, n = tok
                if teng == stream and stream == "pe":
                    continue
                key = "e:" + teng
                if hw.get(key, 0) >= n:
                    continue
                hw[key] = n
                waits.append((self.esem[teng], n))
            else:
                _, nm, v = tok
                key = "d:" + nm
                if hw.get(key, 0) >= v:
                    continue
                hw[key] = v
                waits.append((self.dsem[nm], v))
        return waits

    def _update(self, reads, writes, tok):
        for k in reads:
            R = self._get(k)
            if R["psum"]:
                R["w"] = [tok]
                R["r"] = []
            else:
                R["r"].append(tok)
        for k in writes:
            R = self._get(k)
            R["w"] = [tok]
            R["r"] = []

    def op(self, eng, emit, reads=(), writes=(), track=True):
        waits = self._waits(eng, self._collect(reads, writes))
        if self.DEBUG:
            import sys as _s
            self.oplog.append((eng, self.cnt[eng] + (1 if track else 0),
                               _s._getframe(1).f_lineno, reads, writes,
                               [(w[0].name, w[1]) for w in waits], track))
        sem = self.esem[eng]
        if track:
            self.cnt[eng] += 1

            def run(e, waits=waits, emit=emit, sem=sem):
                for s, v in waits:
                    e.wait_ge(s, v)
                emit(e).then_inc(sem, 1)
        else:
            def run(e, waits=waits, emit=emit):
                for s, v in waits:
                    e.wait_ge(s, v)
                emit(e)

        self.streams[eng].append(run)
        if not track:
            return None
        tok = ("e", eng, self.cnt[eng])
        self._update(reads, writes, tok)
        return tok

    def dma(self, queue, semname, emit, reads=(), writes=()):
        if semname not in self.dsem:
            self.dsem[semname] = self.es.enter_context(
                self.nc.semaphore("ds_" + semname))
            self.dcnt[semname] = 0
        waits = self._waits(queue, self._collect(reads, writes))
        if self.DEBUG:
            import sys as _s
            self.oplog.append((queue + ":" + semname, self.dcnt[semname] + 1,
                               _s._getframe(1).f_lineno, reads, writes,
                               [(w[0].name, w[1]) for w in waits], True))
        self.dcnt[semname] += 1
        target = 16 * self.dcnt[semname]
        sem = self.dsem[semname]

        def run(e, waits=waits, emit=emit, sem=sem):
            for s, v in waits:
                e.wait_ge(s, v)
            emit(e).then_inc(sem, 16)

        self.streams[queue].append(run)
        tok = ("d", semname, target)
        self._update(reads, writes, tok)
        return tok

    def set_floor(self):
        self.floor = ([("e", e, self.cnt[e]) for e in self.ENGINES] +
                      [("d", nm, 16 * self.dcnt[nm]) for nm in self.dcnt])

    def finalize_sp(self):
        deps = [("d", nm, 16 * self.dcnt[nm]) for nm in self.dcnt]
        waits = self._waits("sp", deps)

        def run(e, waits=waits):
            for s, v in waits:
                e.wait_ge(s, v)

        self.streams["sp"].append(run)


def build_nc(phases=("a1", "a2", "b")):
    nc = bass.Bass("TRN2", target_bir_lowering=False, debug=False)

    x_h = nc.dram_tensor("x", [TOK, D], F32, kind="ExternalInput").ap()
    w0_h = nc.dram_tensor("w0", [D, K3], F32, kind="ExternalInput").ap()
    w1_h = nc.dram_tensor("w1", [D, K3], F32, kind="ExternalInput").ap()
    rw_h = nc.dram_tensor("rw", [P, HC * NE], F32,
                          kind="ExternalInput").ap()
    rb_h = nc.dram_tensor("rb", [1, NE], F32, kind="ExternalInput").ap()
    ew_h = nc.dram_tensor("ew", [NE * D, D], F32, kind="ExternalInput").ap()
    eb_h = nc.dram_tensor("eb", [NE, D], F32, kind="ExternalInput").ap()
    id_h = nc.dram_tensor("ident", [P, P], F32, kind="ExternalInput").ap()

    out_h = nc.dram_tensor("out", [TOK, D], F32, kind="ExternalOutput").ap()

    xnthi_h = nc.dram_tensor("xnthi", [NST, P, NTT * D], BF16,
                             kind="Internal").ap()
    h1t_h = nc.dram_tensor("h1t", [NST, P, NTT * D], F32, kind="Internal").ap()
    s_h = nc.dram_tensor("s", [TOK, D], F32, kind="Internal").ap()

    def tokslice(hbm, tok0):
        return hbm[tok0:tok0 + TS, :].rearrange("(a p) h -> p a h", p=P)

    with ExitStack() as es:
        em = Em(nc, es)

        def sbuf(name, shape, dt=F32, stack=None, side=None):
            t = (stack or es).enter_context(
                nc.sbuf_tensor(name, shape, dt, side=side))
            em.region(name)
            return t

        def psum(name, shape, dt=F32):
            t = es.enter_context(nc.psum_tensor(name, shape, dt))
            em.region(name, psum=True)
            return t

        zb = [psum("z%d" % i, [P, 512]) for i in range(6)]
        trb = [psum("tr%d" % i, [P, 512]) for i in range(2)]
        psr = trb[0]

        id_sb = sbuf("idsb", [P, P], F32, side="right")
        rwf = sbuf("rwf", [P, HC * NE], F32, side="right")
        wrh = sbuf("wrh", [P, HC * NE], BF16, side="right")
        wrl = sbuf("wrl", [P, HC * NE], BF16, side="right")
        rbf = sbuf("rbf", [1, NE], F32, side="right")
        rbh = sbuf("rbh", [1, NE], BF16, side="right")
        rbl = sbuf("rbl", [1, NE], BF16, side="right")
        ones_r = sbuf("ones_r", [1, TS], BF16, side="right")
        eb_sb = sbuf("ebsb", [NE, D], F32, side="right")
        ebh_sb = sbuf("ebh", [NE, D], BF16, side="right")
        onehot = sbuf("onehot", [P, (TOK // P) * NE], F32, side="right")
        gate = sbuf("gate", [P, TOK // P], F32, side="right")
        carry = sbuf("carry", [P, 2 * NSEQ * HC], F32, side="right")
        stat = sbuf("stat", [P, 96], F32, side="right")
        logt = sbuf("logt", [NE, TS], F32, side="right")
        logits = sbuf("logits", [P, NTT * NE], F32, side="right")
        ohT = sbuf("ohT", [NE, P], BF16, side="right")
        sq8 = sbuf("sq8", [P, 2 * NE], F32, side="right")

        em.dma("sp", "cid", lambda e: e.dma_start(out=id_sb[:], in_=id_h),
               writes=("idsb",))
        em.dma("sp", "crw", lambda e: e.dma_start(out=rwf[:], in_=rw_h),
               writes=("rwf",))
        em.dma("sp", "crb", lambda e: e.dma_start(out=rbf[:], in_=rb_h),
               writes=("rbf",))
        em.dma("sp", "ceb", lambda e: e.dma_start(out=eb_sb[:], in_=eb_h),
               writes=("ebsb",))
        em.op("dve", lambda e: e.tensor_copy(out=wrh[:], in_=rwf[:]),
              reads=("rwf",), writes=("wrh",))
        em.op("dve", lambda e: e.tensor_sub(wrl[:], rwf[:], wrh[:]),
              reads=("rwf", "wrh"), writes=("wrl",))
        em.op("dve", lambda e: e.tensor_copy(out=rbh[:], in_=rbf[:]),
              reads=("rbf",), writes=("rbh",))
        em.op("dve", lambda e: e.tensor_sub(rbl[:], rbf[:], rbh[:]),
              reads=("rbf", "rbh"), writes=("rbl",))
        em.op("pool", lambda e: e.memset(ones_r[:], 1.0), writes=("ones_r",))
        em.op("dve", lambda e: e.tensor_copy(out=ebh_sb[:], in_=eb_sb[:]),
              reads=("ebsb",), writes=("ebh",))

        es_w0 = ExitStack()
        w0_sb = sbuf("w0sb", [P, HC * K3], BF16, stack=es_w0)
        es_w1 = ExitStack()
        w1_sb = sbuf("w1sb", [P, HC * K3], BF16, stack=es_w1, side="right")
        for c in range(4):
            for (wsb, wh, nm) in ((w0_sb, w0_h, "w0"), (w1_sb, w1_h, "w1")):
                em.dma("pool", "%s%d" % (nm, c),
                       lambda e, wsb=wsb, wh=wh, c=c: e.dma_start(
                           out=wsb[:, c * 2 * K3:(c + 1) * 2 * K3],
                           in_=wh[c * 256:(c + 1) * 256, :].rearrange(
                               "(a p) k -> p a k", p=P)),
                       writes=("%s.%d" % (nm, c),))

        es_a = ExitStack()
        in0 = [sbuf("in0%d" % i, [P, NTT * D], stack=es_a) for i in range(2)]
        t32 = [sbuf("t32%d" % i, [P, NTT * D], stack=es_a) for i in range(2)]
        thi = [sbuf("thi%d" % i, [P, NTT * D], BF16, stack=es_a)
               for i in range(2)]
        tlo = sbuf("tlo", [P, NTT * D], BF16, stack=es_a)
        Cb = sbuf("Cb", [P, NTT * D], stack=es_a)
        SG = sbuf("SGb", [P, NTT * D], stack=es_a)
        HP = sbuf("HPb", [P, NTT * D], stack=es_a)
        SPb = sbuf("SPb", [P, NTT * D], stack=es_a)
        OUT = sbuf("OUTb", [P, NTT * D], stack=es_a)
        CMB = sbuf("CMBb", [P, NTT * D], stack=es_a)
        SQ = sbuf("SQb", [P, D], stack=es_a)

        iters = [(s, k) for k in range(NST_SEQ) for s in range(NSEQ)]

        def ln_part(it, xkey, xb):
            base = (it % 2) * 40
            SK = "statln%d" % (it % 2)
            st_ = stat[:, base:base + 40]
            for tt in range(NTT):
                for hh in range(2):
                    em.op("dve", lambda e, tt=tt, hh=hh: e.bn_stats(
                        out=st_[:, tt * 12 + hh * 6:tt * 12 + hh * 6 + 6],
                        in_=xb[:, tt * D + hh * 512:tt * D + (hh + 1) * 512]),
                        reads=(xkey,), writes=(SK,))
                em.op("dve", lambda e, tt=tt: e.bn_aggr(
                    out=st_[:, 24 + 2 * tt:26 + 2 * tt],
                    in_=st_[:, tt * 12:tt * 12 + 12]),
                    reads=(SK,), writes=(SK,))
            vview = st_[:, 24:28].rearrange("p (a b) -> p b a", b=2)[:, 1]
            em.op("dve", lambda e, vview=vview: e.tensor_scalar(
                out=st_[:, 28:30], in0=vview, scalar1=EPS,
                scalar2=None, op0=ALU.add), reads=(SK,), writes=(SK,))
            em.op("dve", lambda e: e.tensor_scalar(
                out=st_[:, 30:32], in0=st_[:, 28:30], scalar1=-0.5,
                scalar2=1.5, op0=ALU.mult, op1=ALU.add),
                reads=(SK,), writes=(SK,))
            for _ in range(3):
                em.op("dve", lambda e: e.tensor_mul(
                    st_[:, 32:34], st_[:, 30:32], st_[:, 30:32]),
                    reads=(SK,), writes=(SK,))
                em.op("dve", lambda e: e.tensor_mul(
                    st_[:, 32:34], st_[:, 32:34], st_[:, 28:30]),
                    reads=(SK,), writes=(SK,))
                em.op("dve", lambda e: e.tensor_scalar(
                    out=st_[:, 32:34], in0=st_[:, 32:34], scalar1=-0.5,
                    scalar2=1.5, op0=ALU.mult, op1=ALU.add),
                    reads=(SK,), writes=(SK,))
                em.op("dve", lambda e: e.tensor_mul(
                    st_[:, 30:32], st_[:, 30:32], st_[:, 32:34]),
                    reads=(SK,), writes=(SK,))
            for tt in range(NTT):
                xs = xb[:, tt * D:(tt + 1) * D]
                em.op("dve", lambda e, xs=xs, tt=tt: e.tensor_scalar(
                    out=xs, in0=xs, scalar1=st_[:, 24 + 2 * tt:25 + 2 * tt],
                    scalar2=st_[:, 30 + tt:31 + tt],
                    op0=ALU.subtract, op1=ALU.mult),
                    reads=(xkey, SK), writes=(xkey,))

        def tr_part(it, xkey, xb, tkey, dst32):
            blocks = [(tt, hc) for tt in range(NTT) for hc in range(HC)]
            d3 = dst32[:].rearrange("p (h t) -> p h t", h=HC)
            for bi in range(4):
                bank = trb[bi % 2]
                bkey = "tr%d" % (bi % 2)
                tt0, hc0 = blocks[bi * 4]
                for j in range(4):
                    tt, hc = blocks[bi * 4 + j]
                    em.op("pe", lambda e, tt=tt, hc=hc, bank=bank, j=j:
                          e.transpose(out=bank[:, j * P:(j + 1) * P],
                                      in_=xb[:, tt * D + hc * P:
                                             tt * D + (hc + 1) * P],
                                      identity=id_sb[:]),
                          reads=(xkey, "idsb"), writes=(bkey,),
                          track=(j == 3))
                em.op("act", lambda e, bank=bank, tt0=tt0, hc0=hc0:
                      e.copy(out=d3[:, hc0:hc0 + 4, tt0 * P:(tt0 + 1) * P],
                             in_=bank[:]),
                      reads=(bkey,), writes=(tkey,))

        def router(it, s, k):
            cur = it % 2
            th, tl = thi[cur], tlo
            thk = "thi%d" % cur
            em.op("pe", lambda e: e.matmul(
                out=psr[0:NE, 0:TS], lhsT=rbh[:], rhs=ones_r[:],
                start=True, stop=False),
                reads=("rbh", "ones_r"), writes=("tr0",), track=False)
            em.op("pe", lambda e: e.matmul(
                out=psr[0:NE, 0:TS], lhsT=rbl[:], rhs=ones_r[:],
                start=False, stop=False),
                reads=("rbl", "ones_r"), writes=("tr0",), track=False)
            n = 0
            for hc in range(HC):
                for (a, b, ak, bk) in ((wrh, th, "wrh", thk),
                                       (wrh, tl, "wrh", "tlo"),
                                       (wrl, th, "wrl", thk)):
                    n += 1
                    last = (n == 24)
                    em.op("pe", lambda e, a=a, b=b, hc=hc, last=last:
                          e.matmul(out=psr[0:NE, 0:TS],
                                   lhsT=a[:, hc * NE:(hc + 1) * NE],
                                   rhs=b[:, hc * TS:(hc + 1) * TS],
                                   start=False, stop=last),
                          reads=(ak, bk), writes=("tr0",), track=last)
            em.op("act", lambda e: e.copy(out=logt[:], in_=psr[0:NE, 0:TS]),
                  reads=("tr0",), writes=("logt",))
            for tt in range(NTT):
                em.op("pe", lambda e, tt=tt: e.transpose(
                    out=zb[5][:, tt * NE:(tt + 1) * NE],
                    in_=logt[:, tt * P:(tt + 1) * P],
                    identity=id_sb[0:NE, 0:NE]),
                    reads=("logt", "idsb"), writes=("z5",))
            em.op("dve", lambda e: e.tensor_copy(
                out=logits[:, 0:NTT * NE], in_=zb[5][:, 0:NTT * NE]),
                reads=("z5",), writes=("logits",))
            for tt in range(NTT):
                j = (s * TSEQ + k * TS) // P + tt
                lg = logits[:, tt * NE:(tt + 1) * NE]
                c0 = 24 + 2 * tt
                em.op("dve", lambda e, lg=lg, c0=c0: e.tensor_reduce(
                    out=stat[:, c0:c0 + 1], in_=lg, axis=AX.X,
                    op=ALU.max, negate=True),
                    reads=("logits",), writes=("statr",))
                em.op("dve", lambda e, lg=lg, c0=c0, j=j: e.tensor_scalar(
                    out=onehot[:, j * NE:(j + 1) * NE], in0=lg,
                    scalar1=stat[:, c0:c0 + 1], scalar2=0.0,
                    op0=ALU.add, op1=ALU.is_equal),
                    reads=("logits", "statr"), writes=("onehot",))
                em.op("act", lambda e, lg=lg, c0=c0: e.activation(
                    out=sq8[:, 0:NE], in_=lg, func=AF.Sigmoid,
                    bias=stat[:, c0:c0 + 1]),
                    reads=("logits", "statr"), writes=("sq8",))
                em.op("dve", lambda e: e.tensor_scalar(
                    out=sq8[:, NE:2 * NE], in0=sq8[:, 0:NE], scalar1=-1.0,
                    scalar2=1.0, op0=ALU.mult, op1=ALU.add),
                    reads=("sq8",), writes=("sq8",))
                em.op("dve", lambda e: e.reciprocal(
                    out=sq8[:, NE:2 * NE], in_=sq8[:, NE:2 * NE]),
                    reads=("sq8",), writes=("sq8",))
                em.op("dve", lambda e: e.tensor_mul(
                    sq8[:, NE:2 * NE], sq8[:, 0:NE], sq8[:, NE:2 * NE]),
                    reads=("sq8",), writes=("sq8",))
                em.op("dve", lambda e, c0=c0: e.reduce_sum(
                    out=stat[:, c0 + 1:c0 + 2], in_=sq8[:, NE:2 * NE],
                    axis=AX.X), reads=("sq8",), writes=("statr",))
                em.op("dve", lambda e, c0=c0, j=j: e.reciprocal(
                    out=gate[:, j:j + 1], in_=stat[:, c0 + 1:c0 + 2]),
                    reads=("statr",), writes=("gate",))

        def z_part(wsb, wname, rkey, rhs):
            for r in range(24):
                bank = zb[r % 6]
                bkey = "z%d" % (r % 6)
                sl = (r // 6) % 2
                outap = bank[:, sl * TS:(sl + 1) * TS]
                for hc in range(HC):
                    em.op("pe", lambda e, r=r, hc=hc, outap=outap: e.matmul(
                        out=outap,
                        lhsT=wsb[:, hc * K3 + r * P: hc * K3 + (r + 1) * P],
                        rhs=rhs[:, hc * TS:(hc + 1) * TS],
                        start=(hc == 0), stop=(hc == HC - 1)),
                        reads=("%s.%d" % (wname, hc // 2), rkey),
                        writes=(bkey,), track=(hc == HC - 1))
                if r < 8:
                    fg = r
                    em.op("act", lambda e, outap=outap, fg=fg: e.activation(
                        out=SG[:, fg * TS:(fg + 1) * TS], in_=outap,
                        func=AF.Sigmoid), reads=(bkey,), writes=("SGb",))
                    em.op("act", lambda e, outap=outap, fg=fg: e.activation(
                        out=HP[:, fg * TS:(fg + 1) * TS], in_=outap,
                        func=AF.Copy, bias=0.5),
                        reads=(bkey,), writes=("HPb",))
                elif r < 16:
                    fg = r - 8
                    em.op("act", lambda e, outap=outap, fg=fg: e.activation(
                        out=Cb[:, fg * TS:(fg + 1) * TS], in_=outap,
                        func=AF.Sigmoid, scale=-1.0),
                        reads=(bkey,), writes=("Cb",))
                else:
                    fg = r - 16
                    em.op("act", lambda e, outap=outap, fg=fg: e.activation(
                        out=SPb[:, fg * TS:(fg + 1) * TS], in_=outap,
                        func=AF.Sigmoid), reads=(bkey,), writes=("SPb",))
        def post_part(s, k, layer, skey, src32):
            em.op("dve", lambda e: e.tensor_max(HP[:], SG[:], HP[:]),
                  reads=("SGb", "HPb"), writes=("HPb",))
            em.op("dve", lambda e: e.scalar_tensor_tensor(
                out=HP[:], in0=Cb[:], scalar=1.0, in1=HP[:],
                op0=ALU.subtract, op1=ALU.mult),
                reads=("Cb", "HPb"), writes=("HPb",))
            cbase = layer * NSEQ * HC + s * HC
            for fg in range(HC):
                if k == 0:
                    init = 0.0
                    creads = ("Cb", "HPb")
                else:
                    init = carry[:, cbase + fg:cbase + fg + 1]
                    creads = ("Cb", "HPb", "carry")
                em.op("dve", lambda e, fg=fg, init=init: e.tensor_tensor_scan(
                    out=OUT[:, fg * TS:(fg + 1) * TS],
                    data0=Cb[:, fg * TS:(fg + 1) * TS],
                    data1=HP[:, fg * TS:(fg + 1) * TS],
                    initial=init, op0=ALU.mult, op1=ALU.add),
                    reads=creads, writes=("OUTb",))
            em.op("dve", lambda e, cbase=cbase: e.tensor_copy(
                out=carry[:, cbase:cbase + HC],
                in_=OUT[:].rearrange("p (h t) -> p h t", h=HC)[:, :,
                                                             TS - 1:TS]),
                reads=("OUTb",), writes=("carry",))
            em.op("dve", lambda e: e.tensor_add(OUT[:], OUT[:], src32[:]),
                  reads=("OUTb", skey), writes=("OUTb",))
            em.op("dve", lambda e: e.tensor_mul(OUT[:], SPb[:], OUT[:]),
                  reads=("SPb", "OUTb"), writes=("OUTb",))
            em.op("dve", lambda e: e.tensor_sub(CMB[:], src32[:], OUT[:]),
                  reads=(skey, "OUTb"), writes=("CMBb",))

        if "a1" in phases:
            NIT = len(iters)

            def a1_load(it):
                s2, k2 = iters[it]
                em.dma("sp", "in0%s" % "ab"[it % 2],
                       lambda e, s2=s2, k2=k2, it=it: e.dma_start(
                           out=in0[it % 2][:].rearrange(
                               "p (a h) -> p a h", a=NTT),
                           in_=tokslice(x_h, s2 * TSEQ + k2 * TS)),
                       writes=("in0%d" % (it % 2),))

            def a1_cast(it):
                cur = it % 2
                st = iters[it][0] * NST_SEQ + iters[it][1]
                em.op("act", lambda e, cur=cur: e.copy(
                    out=thi[cur][:], in_=t32[cur][:]),
                    reads=("t32%d" % cur,), writes=("thi%d" % cur,))
                em.op("dve", lambda e, cur=cur: e.tensor_sub(
                    tlo[:], t32[cur][:], thi[cur][:]),
                    reads=("t32%d" % cur, "thi%d" % cur), writes=("tlo",))
                em.dma("pool", "sthi%s" % "ab"[cur],
                       lambda e, cur=cur, st=st: e.dma_start(
                           out=xnthi_h[st], in_=thi[cur][:]),
                       reads=("thi%d" % cur,), writes=("hbm.xnthi.%d" % st,))

            for it in range(-2, NIT):
                if 0 <= it + 2 < NIT:
                    a1_load(it + 2)
                    ln_part(it + 2, "in0%d" % ((it + 2) % 2), in0[(it + 2) % 2])
                if 0 <= it + 1 < NIT:
                    c1 = (it + 1) % 2
                    tr_part(it + 1, "in0%d" % c1, in0[c1], "t32%d" % c1,
                            t32[c1])
                    a1_cast(it + 1)
                if 0 <= it:
                    cur = it % 2
                    z_part(w0_sb, "w0", "thi%d" % cur, thi[cur])
                    s1, k1 = iters[it]
                    st = s1 * NST_SEQ + k1
                    post_part(s1, k1, 0, "t32%d" % (it % 2), t32[it % 2])
                    em.dma("pool", "scmb", lambda e, st=st: e.dma_start(
                        out=h1t_h[st], in_=CMB[:]),
                        reads=("CMBb",), writes=("hbm.h1t.%d" % st,))
                if 0 <= it + 1 < NIT:
                    s2, k2 = iters[it + 1]
                    router(it + 1, s2, k2)

        if "a2" in phases:
            NIT = len(iters)

            rot = [in0[0], in0[1], t32[1]]
            rkeys = ["in00", "in01", "t321"]

            def a2_load(it):
                s2, k2 = iters[it]
                st2 = s2 * NST_SEQ + k2
                em.dma("sp", "in0%s" % "abc"[it % 3],
                       lambda e, st2=st2, it=it: e.dma_start(
                           out=rot[it % 3][:], in_=h1t_h[st2]),
                       reads=("hbm.h1t.%d" % st2,),
                       writes=(rkeys[it % 3],))

            def a2_trback(it):
                s1, k1 = iters[it]
                stok, stokkey = t32[0], "t320"
                s3 = stok[:].rearrange("p (a h) -> p a h", a=NTT)
                blocks = [(tt, hc) for tt in range(NTT) for hc in range(HC)]
                for bi in range(4):
                    bank = trb[bi % 2]
                    bkey = "tr%d" % (bi % 2)
                    tt0, hc0 = blocks[bi * 4]
                    for j in range(4):
                        tt, hc = blocks[bi * 4 + j]
                        em.op("pe", lambda e, tt=tt, hc=hc, bank=bank, j=j:
                              e.transpose(
                                  out=bank[:, j * P:(j + 1) * P],
                                  in_=CMB[:, hc * TS + tt * P:
                                          hc * TS + (tt + 1) * P],
                                  identity=id_sb[:]),
                              reads=("CMBb", "idsb"), writes=(bkey,),
                              track=(j == 3))
                    em.op("act", lambda e, bank=bank, tt0=tt0, hc0=hc0:
                          e.copy(out=s3[:, tt0, hc0 * P:(hc0 + 4) * P],
                                 in_=bank[:]),
                          reads=(bkey,), writes=(stokkey,))
                em.dma("pool", "sst", lambda e, s1=s1, k1=k1: e.dma_start(
                    out=tokslice(s_h, s1 * TSEQ + k1 * TS), in_=s3),
                    reads=(stokkey,),
                    writes=("hbm.s.%d" % (s1 * TSEQ + k1 * TS),))

            def a2_cast(it):
                em.op("dve", lambda e, it=it: e.tensor_copy(
                    out=thi[it % 2][:], in_=rot[it % 3][:]),
                    reads=(rkeys[it % 3],), writes=("thi%d" % (it % 2),))

            for it in range(0, NIT + 1):
                if it < NIT:
                    if it == 0:
                        a2_load(0)
                        a2_load(1)
                        a2_cast(0)
                    if it + 2 < NIT:
                        a2_load(it + 2)
                    if it + 1 < NIT:
                        a2_cast(it + 1)
                    cur = it % 2
                    z_part(w1_sb, "w1", "thi%d" % cur, thi[cur])
                if it - 1 >= 0:
                    a2_trback(it - 1)
                if it < NIT:
                    s1, k1 = iters[it]
                    post_part(s1, k1, 1, rkeys[it % 3], rot[it % 3])

        if "b" in phases:
            es_a.close()
            es_w0.close()
            es_w1.close()
            em.set_floor()
            exp = [sbuf("exp%d" % e2, [P, HC * D], BF16) for e2 in range(NE)]
            bx = [sbuf("bx%d" % i, [P, NTT * D]) for i in range(2)]
            bs = [sbuf("bs%d" % i, [P, NTT * D]) for i in range(2)]
            bh = [sbuf("bh%d" % i, [P, NTT * D], BF16) for i in range(2)]
            bo = [sbuf("bo%d" % i, [P, NTT * D]) for i in range(2)]
            bacc = [sbuf("bacc%d" % i, [P, D]) for i in range(2)]
            bg = sbuf("bg", [P, D])
            for e2 in range(NE):
                em.dma("pool", "exp%d" % e2, lambda e, e2=e2: e.dma_start(
                    out=exp[e2][:],
                    in_=ew_h[e2 * D:(e2 + 1) * D, :].rearrange(
                        "(c p) d -> p c d", p=P)),
                    writes=("exp%d" % e2,))

            def bload(bst, slot):
                tok0 = bst * TS
                s = tok0 // TSEQ
                k = (tok0 % TSEQ) // TS
                stA = s * NST_SEQ + k
                em.dma("sp", "bx%s" % "ab"[slot],
                       lambda e, tok0=tok0, slot=slot: e.dma_start(
                           out=bx[slot][:].rearrange(
                               "p (a h) -> p a h", a=NTT),
                           in_=tokslice(x_h, tok0)),
                       writes=("bx%d" % slot,))
                em.dma("sp", "bs%s" % "ab"[slot],
                       lambda e, tok0=tok0, slot=slot: e.dma_start(
                           out=bs[slot][:].rearrange(
                               "p (a h) -> p a h", a=NTT),
                           in_=tokslice(s_h, tok0)),
                       reads=("hbm.s.%d" % tok0,),
                       writes=("bs%d" % slot,))
                em.dma("sp", "bh%s" % "ab"[slot],
                       lambda e, stA=stA, slot=slot: e.dma_start(
                           out=bh[slot][:], in_=xnthi_h[stA]),
                       reads=("hbm.xnthi.%d" % stA,),
                       writes=("bh%d" % slot,))

            bload(0, 0)
            pairbanks = [[0, 1, 2, 3], [4, 5, 6, 7]]
            allb = zb + trb
            bkeys = ["z%d" % i for i in range(6)] + ["tr0", "tr1"]

            def bias_part(j):
                bslot = j % 2
                bk = "bacc%d" % bslot
                em.op("pe", lambda e, j=j: e.transpose(
                    out=trb[0][0:NE, 0:P],
                    in_=onehot[:, j * NE:(j + 1) * NE],
                    identity=id_sb[:]),
                    reads=("onehot", "idsb"), writes=("tr0",))
                em.op("act", lambda e: e.copy(
                    out=ohT[:], in_=trb[0][0:NE, 0:P]),
                    reads=("tr0",), writes=("ohT",))
                for g in range(2):
                    em.op("pe", lambda e, g=g: e.matmul(
                        out=trb[g][:], lhsT=ohT[:],
                        rhs=ebh_sb[:, g * 512:(g + 1) * 512],
                        start=True, stop=True),
                        reads=("ohT", "ebh"), writes=("tr%d" % g,))
                for g in range(2):
                    em.op("dve", lambda e, g=g, bslot=bslot: e.tensor_copy(
                        out=bacc[bslot][:, g * 512:(g + 1) * 512],
                        in_=trb[g][:]),
                        reads=("tr%d" % g,), writes=(bk,))

            bias_part(0)
            for bst in range(NST):
                cur = bst % 2
                if bst + 1 < NST:
                    bload(bst + 1, (bst + 1) % 2)
                hkey = "bh%d" % cur
                for tt in range(NTT):
                    j = bst * NTT + tt
                    cj = j % 2
                    bk = "bacc%d" % cj
                    for ep in range(NE // 2):
                        grp = pairbanks[ep % 2]
                        if ep == 3 and j + 1 < NST * NTT:
                            bias_part(j + 1)
                        for ei in range(2):
                            e2 = ep * 2 + ei
                            for g in range(2):
                                bi = grp[ei * 2 + g]
                                for hc in range(HC):
                                    em.op("pe", lambda e, e2=e2, hc=hc, g=g,
                                          bank=allb[bi], tt=tt, cur=cur:
                                          e.matmul(
                                              out=bank[:, 0:512],
                                              lhsT=bh[cur][
                                                  :, hc * TS + tt * P:
                                                  hc * TS + (tt + 1) * P],
                                              rhs=exp[e2][
                                                  :, hc * D + g * 512:
                                                  hc * D + (g + 1) * 512],
                                              start=(hc == 0),
                                              stop=(hc == HC - 1)),
                                          reads=(hkey, "exp%d" % e2),
                                          writes=(bkeys[bi],),
                                          track=(hc == HC - 1))
                        for ei in range(2):
                            e2 = ep * 2 + ei
                            oh_col = onehot[:, j * NE + e2:j * NE + e2 + 1]
                            for g in range(2):
                                bi = grp[ei * 2 + g]
                                em.op("dve", lambda e, bank=allb[bi], g=g,
                                      oh_col=oh_col, cj=cj:
                                      e.scalar_tensor_tensor(
                                          out=bacc[cj][:,
                                                       g * 512:(g + 1) * 512],
                                          in0=bank[:, 0:512],
                                          scalar=oh_col,
                                          in1=bacc[cj][:,
                                                       g * 512:(g + 1) * 512],
                                          op0=ALU.mult, op1=ALU.add),
                                      reads=(bkeys[bi], "onehot", bk),
                                      writes=(bk,))
                    em.op("dve", lambda e, j=j, cj=cj: e.tensor_scalar(
                        out=bacc[cj][:], in0=bacc[cj][:],
                        scalar1=gate[:, j:j + 1],
                        scalar2=None, op0=ALU.mult),
                        reads=(bk, "gate"), writes=(bk,))
                    em.op("act", lambda e, cj=cj: e.activation(
                        out=bg[:], in_=bacc[cj][:], func=AF.Silu),
                        reads=(bk,), writes=("bg",))
                    em.op("dve", lambda e, tt=tt, cur=cur: e.tensor_mul(
                        bg[:], bs[cur][:, tt * D:(tt + 1) * D], bg[:]),
                        reads=("bs%d" % cur, "bg"), writes=("bg",))
                    em.op("pool", lambda e, tt=tt, cur=cur: e.tensor_add(
                        bo[cur][:, tt * D:(tt + 1) * D],
                        bx[cur][:, tt * D:(tt + 1) * D], bg[:]),
                        reads=("bx%d" % cur, "bg"), writes=("bo%d" % cur,))
                em.dma("pool", "bo%s" % "ab"[cur],
                       lambda e, bst=bst, cur=cur: e.dma_start(
                           out=tokslice(out_h, bst * TS),
                           in_=bo[cur][:].rearrange(
                               "p (a h) -> p a h", a=NTT)),
                       reads=("bo%d" % cur,))

        em.finalize_sp()

        block = es.enter_context(nc.Block())

        @block.sync
        def _(e):
            for f in em.streams["sp"]:
                f(e)

        @block.tensor
        def _(e):
            for f in em.streams["pe"]:
                f(e)

        @block.scalar
        def _(e):
            for f in em.streams["act"]:
                f(e)

        @block.vector
        def _(e):
            for f in em.streams["dve"]:
                f(e)

        @block.gpsimd
        def _(e):
            for f in em.streams["pool"]:
                f(e)

    nc._em_oplog = em.oplog
    return nc


def kernel(L0_attn_h, L0_attn_W0, L0_attn_W1, L0_ffn_router_w,
           L0_ffn_router_b, L0_ffn_expert_w, L0_ffn_expert_b):
    x = np.ascontiguousarray(np.asarray(L0_attn_h, np.float32)
                             ).reshape(16 * TSEQ, D)
    w0 = np.ascontiguousarray(np.asarray(L0_attn_W0, np.float32))
    w1 = np.ascontiguousarray(np.asarray(L0_attn_W1, np.float32))
    rw = np.ascontiguousarray(
        np.asarray(L0_ffn_router_w, np.float32).reshape(HC, P, NE)
        .transpose(1, 0, 2).reshape(P, HC * NE))
    rb = np.ascontiguousarray(np.asarray(L0_ffn_router_b, np.float32)
                              ).reshape(1, NE)
    ew = np.ascontiguousarray(np.asarray(L0_ffn_expert_w, np.float32)
                              ).reshape(NE * D, D)
    eb = np.ascontiguousarray(np.asarray(L0_ffn_expert_b, np.float32))
    ident = np.eye(P, dtype=np.float32)

    nc = build_nc()
    in_maps = []
    for c in range(N_CORES):
        in_maps.append(dict(x=x[c * TOK:(c + 1) * TOK], w0=w0, w1=w1,
                            rw=rw, rb=rb, ew=ew, eb=eb, ident=ident))
    res = run_bass_kernel_spmd(nc, in_maps, core_ids=list(range(N_CORES)))
    out = np.concatenate([res.results[c]["out"] for c in range(N_CORES)],
                         axis=0)
    return out.reshape(16, TSEQ, D)


# version 38 (final)
# speedup vs baseline: 1.2144x; 1.0088x over previous
import numpy as np
from contextlib import ExitStack

import concourse.bass as bass
import concourse.mybir as mybir
from concourse.bass_utils import run_bass_kernel_spmd

F32 = mybir.dt.float32
BF16 = mybir.dt.bfloat16
AF = mybir.ActivationFunctionType
ALU = mybir.AluOpType
AX = mybir.AxisListType

P = 128
D = 1024
K3 = 3 * D
HC = D // P
NE = 8
NSEQ = 2
TSEQ = 4096
TOK = NSEQ * TSEQ
TS = 256
NTT = TS // P
NST_SEQ = TSEQ // TS
NST = NSEQ * NST_SEQ
EPS = 1e-5
N_CORES = 8


class Em:

    ENGINES = ("pe", "act", "dve", "pool")
    DEBUG = False

    def __init__(self, nc, es):
        self.oplog = []
        self.nc = nc
        self.streams = {"pe": [], "act": [], "dve": [], "pool": [], "sp": []}
        self.cnt = {e: 0 for e in self.ENGINES}
        self.esem = {e: es.enter_context(nc.semaphore("es_" + e))
                     for e in self.ENGINES}
        self.es = es
        self.dsem = {}
        self.dcnt = {}
        self.hw = {s: {} for s in self.streams}
        self.reg = {}
        self.floor = []

    def region(self, key, psum=False):
        self.reg[key] = {"w": [], "r": [], "psum": psum}

    def _get(self, key):
        if key not in self.reg:
            self.region(key)
        return self.reg[key]

    def _collect(self, reads, writes):
        deps = list(self.floor)
        for k in reads:
            R = self._get(k)
            deps += R["w"] + (R["r"] if R["psum"] else [])
        for k in writes:
            R = self._get(k)
            deps += R["w"] + R["r"]
        return deps

    def _waits(self, stream, deps):
        hw = self.hw[stream]
        waits = []
        for tok in deps:
            if tok is None:
                continue
            if tok[0] == "e":
                _, teng, n = tok
                if teng == stream and stream == "pe":
                    continue
                key = "e:" + teng
                if hw.get(key, 0) >= n:
                    continue
                hw[key] = n
                waits.append((self.esem[teng], n))
            else:
                _, nm, v = tok
                key = "d:" + nm
                if hw.get(key, 0) >= v:
                    continue
                hw[key] = v
                waits.append((self.dsem[nm], v))
        return waits

    def _update(self, reads, writes, tok):
        for k in reads:
            R = self._get(k)
            if R["psum"]:
                R["w"] = [tok]
                R["r"] = []
            else:
                R["r"].append(tok)
        for k in writes:
            R = self._get(k)
            R["w"] = [tok]
            R["r"] = []

    def op(self, eng, emit, reads=(), writes=(), track=True):
        waits = self._waits(eng, self._collect(reads, writes))
        if self.DEBUG:
            import sys as _s
            self.oplog.append((eng, self.cnt[eng] + (1 if track else 0),
                               _s._getframe(1).f_lineno, reads, writes,
                               [(w[0].name, w[1]) for w in waits], track))
        sem = self.esem[eng]
        if track:
            self.cnt[eng] += 1

            def run(e, waits=waits, emit=emit, sem=sem):
                for s, v in waits:
                    e.wait_ge(s, v)
                emit(e).then_inc(sem, 1)
        else:
            def run(e, waits=waits, emit=emit):
                for s, v in waits:
                    e.wait_ge(s, v)
                emit(e)

        self.streams[eng].append(run)
        if not track:
            return None
        tok = ("e", eng, self.cnt[eng])
        self._update(reads, writes, tok)
        return tok

    def dma(self, queue, semname, emit, reads=(), writes=()):
        if semname not in self.dsem:
            self.dsem[semname] = self.es.enter_context(
                self.nc.semaphore("ds_" + semname))
            self.dcnt[semname] = 0
        waits = self._waits(queue, self._collect(reads, writes))
        if self.DEBUG:
            import sys as _s
            self.oplog.append((queue + ":" + semname, self.dcnt[semname] + 1,
                               _s._getframe(1).f_lineno, reads, writes,
                               [(w[0].name, w[1]) for w in waits], True))
        self.dcnt[semname] += 1
        target = 16 * self.dcnt[semname]
        sem = self.dsem[semname]

        def run(e, waits=waits, emit=emit, sem=sem):
            for s, v in waits:
                e.wait_ge(s, v)
            emit(e).then_inc(sem, 16)

        self.streams[queue].append(run)
        tok = ("d", semname, target)
        self._update(reads, writes, tok)
        return tok

    def set_floor(self):
        self.floor = ([("e", e, self.cnt[e]) for e in self.ENGINES] +
                      [("d", nm, 16 * self.dcnt[nm]) for nm in self.dcnt])

    def finalize_sp(self):
        deps = [("d", nm, 16 * self.dcnt[nm]) for nm in self.dcnt]
        waits = self._waits("sp", deps)

        def run(e, waits=waits):
            for s, v in waits:
                e.wait_ge(s, v)

        self.streams["sp"].append(run)


def build_nc(phases=("a1", "a2", "b")):
    nc = bass.Bass("TRN2", target_bir_lowering=False, debug=False)

    x_h = nc.dram_tensor("x", [TOK, D], F32, kind="ExternalInput").ap()
    w0_h = nc.dram_tensor("w0", [D, K3], F32, kind="ExternalInput").ap()
    w1_h = nc.dram_tensor("w1", [D, K3], F32, kind="ExternalInput").ap()
    rw_h = nc.dram_tensor("rw", [P, HC * NE], F32,
                          kind="ExternalInput").ap()
    rb_h = nc.dram_tensor("rb", [1, NE], F32, kind="ExternalInput").ap()
    ew_h = nc.dram_tensor("ew", [NE * D, D], F32, kind="ExternalInput").ap()
    eb_h = nc.dram_tensor("eb", [NE, D], F32, kind="ExternalInput").ap()
    id_h = nc.dram_tensor("ident", [P, P], F32, kind="ExternalInput").ap()

    out_h = nc.dram_tensor("out", [TOK, D], F32, kind="ExternalOutput").ap()

    xnthi_h = nc.dram_tensor("xnthi", [NST, P, NTT * D], BF16,
                             kind="Internal").ap()
    h1t_h = nc.dram_tensor("h1t", [NST, P, NTT * D], F32, kind="Internal").ap()
    s_h = nc.dram_tensor("s", [TOK, D], F32, kind="Internal").ap()

    def tokslice(hbm, tok0):
        return hbm[tok0:tok0 + TS, :].rearrange("(a p) h -> p a h", p=P)

    with ExitStack() as es:
        em = Em(nc, es)

        def sbuf(name, shape, dt=F32, stack=None, side=None):
            t = (stack or es).enter_context(
                nc.sbuf_tensor(name, shape, dt, side=side))
            em.region(name)
            return t

        def psum(name, shape, dt=F32):
            t = es.enter_context(nc.psum_tensor(name, shape, dt))
            em.region(name, psum=True)
            return t

        zb = [psum("z%d" % i, [P, 512]) for i in range(6)]
        trb = [psum("tr%d" % i, [P, 512]) for i in range(2)]
        psr = trb[0]

        id_sb = sbuf("idsb", [P, P], F32, side="right")
        rwf = sbuf("rwf", [P, HC * NE], F32, side="right")
        wrh = sbuf("wrh", [P, HC * NE], BF16, side="right")
        wrl = sbuf("wrl", [P, HC * NE], BF16, side="right")
        rbf = sbuf("rbf", [1, NE], F32, side="right")
        rbh = sbuf("rbh", [1, NE], BF16, side="right")
        rbl = sbuf("rbl", [1, NE], BF16, side="right")
        ones_r = sbuf("ones_r", [1, TS], BF16, side="right")
        eb_sb = sbuf("ebsb", [NE, D], F32, side="right")
        ebh_sb = sbuf("ebh", [NE, D], BF16, side="right")
        onehot = sbuf("onehot", [P, (TOK // P) * NE], F32, side="right")
        gate = sbuf("gate", [P, TOK // P], F32, side="right")
        carry = sbuf("carry", [P, 2 * NSEQ * HC], F32, side="right")
        stat = sbuf("stat", [P, 96], F32, side="right")
        logt = sbuf("logt", [NE, TS], F32, side="right")
        logits = sbuf("logits", [P, NTT * NE], F32, side="right")
        ohT = sbuf("ohT", [NE, P], BF16, side="right")
        sq8 = sbuf("sq8", [P, 2 * NE], F32, side="right")

        em.dma("sp", "cid", lambda e: e.dma_start(out=id_sb[:], in_=id_h),
               writes=("idsb",))
        em.dma("sp", "crw", lambda e: e.dma_start(out=rwf[:], in_=rw_h),
               writes=("rwf",))
        em.dma("sp", "crb", lambda e: e.dma_start(out=rbf[:], in_=rb_h),
               writes=("rbf",))
        em.dma("sp", "ceb", lambda e: e.dma_start(out=eb_sb[:], in_=eb_h),
               writes=("ebsb",))
        em.op("dve", lambda e: e.tensor_copy(out=wrh[:], in_=rwf[:]),
              reads=("rwf",), writes=("wrh",))
        em.op("dve", lambda e: e.tensor_sub(wrl[:], rwf[:], wrh[:]),
              reads=("rwf", "wrh"), writes=("wrl",))
        em.op("dve", lambda e: e.tensor_copy(out=rbh[:], in_=rbf[:]),
              reads=("rbf",), writes=("rbh",))
        em.op("dve", lambda e: e.tensor_sub(rbl[:], rbf[:], rbh[:]),
              reads=("rbf", "rbh"), writes=("rbl",))
        em.op("pool", lambda e: e.memset(ones_r[:], 1.0), writes=("ones_r",))
        em.op("dve", lambda e: e.tensor_copy(out=ebh_sb[:], in_=eb_sb[:]),
              reads=("ebsb",), writes=("ebh",))

        es_w0 = ExitStack()
        w0_sb = sbuf("w0sb", [P, HC * K3], BF16, stack=es_w0)
        es_w1 = ExitStack()
        w1_sb = sbuf("w1sb", [P, HC * K3], BF16, stack=es_w1, side="right")
        for c in range(4):
            for (wsb, wh, nm) in ((w0_sb, w0_h, "w0"), (w1_sb, w1_h, "w1")):
                em.dma("pool", "%s%d" % (nm, c),
                       lambda e, wsb=wsb, wh=wh, c=c: e.dma_start(
                           out=wsb[:, c * 2 * K3:(c + 1) * 2 * K3],
                           in_=wh[c * 256:(c + 1) * 256, :].rearrange(
                               "(a p) k -> p a k", p=P)),
                       writes=("%s.%d" % (nm, c),))

        es_a = ExitStack()
        in0 = [sbuf("in0%d" % i, [P, NTT * D], stack=es_a) for i in range(2)]
        t32 = [sbuf("t32%d" % i, [P, NTT * D], stack=es_a) for i in range(2)]
        thi = [sbuf("thi%d" % i, [P, NTT * D], BF16, stack=es_a)
               for i in range(2)]
        tlo = sbuf("tlo", [P, NTT * D], BF16, stack=es_a)
        Cb = sbuf("Cb", [P, NTT * D], stack=es_a)
        SG = sbuf("SGb", [P, NTT * D], stack=es_a)
        HP = sbuf("HPb", [P, NTT * D], stack=es_a)
        SPb = sbuf("SPb", [P, NTT * D], stack=es_a)
        OUT = sbuf("OUTb", [P, NTT * D], stack=es_a)
        CMB = sbuf("CMBb", [P, NTT * D], stack=es_a)
        SQ = sbuf("SQb", [P, D], stack=es_a)

        iters = [(s, k) for k in range(NST_SEQ) for s in range(NSEQ)]

        def ln_part(it, xkey, xb):
            base = (it % 2) * 40
            SK = "statln%d" % (it % 2)
            st_ = stat[:, base:base + 40]
            for tt in range(NTT):
                for hh in range(2):
                    em.op("dve", lambda e, tt=tt, hh=hh: e.bn_stats(
                        out=st_[:, tt * 12 + hh * 6:tt * 12 + hh * 6 + 6],
                        in_=xb[:, tt * D + hh * 512:tt * D + (hh + 1) * 512]),
                        reads=(xkey,), writes=(SK,))
                em.op("dve", lambda e, tt=tt: e.bn_aggr(
                    out=st_[:, 24 + 2 * tt:26 + 2 * tt],
                    in_=st_[:, tt * 12:tt * 12 + 12]),
                    reads=(SK,), writes=(SK,))
            vview = st_[:, 24:28].rearrange("p (a b) -> p b a", b=2)[:, 1]
            em.op("dve", lambda e, vview=vview: e.tensor_scalar(
                out=st_[:, 28:30], in0=vview, scalar1=EPS,
                scalar2=None, op0=ALU.add), reads=(SK,), writes=(SK,))
            em.op("dve", lambda e: e.tensor_scalar(
                out=st_[:, 30:32], in0=st_[:, 28:30], scalar1=-0.5,
                scalar2=1.5, op0=ALU.mult, op1=ALU.add),
                reads=(SK,), writes=(SK,))
            for _ in range(3):
                em.op("dve", lambda e: e.tensor_mul(
                    st_[:, 32:34], st_[:, 30:32], st_[:, 30:32]),
                    reads=(SK,), writes=(SK,))
                em.op("dve", lambda e: e.tensor_mul(
                    st_[:, 32:34], st_[:, 32:34], st_[:, 28:30]),
                    reads=(SK,), writes=(SK,))
                em.op("dve", lambda e: e.tensor_scalar(
                    out=st_[:, 32:34], in0=st_[:, 32:34], scalar1=-0.5,
                    scalar2=1.5, op0=ALU.mult, op1=ALU.add),
                    reads=(SK,), writes=(SK,))
                em.op("dve", lambda e: e.tensor_mul(
                    st_[:, 30:32], st_[:, 30:32], st_[:, 32:34]),
                    reads=(SK,), writes=(SK,))
            for tt in range(NTT):
                xs = xb[:, tt * D:(tt + 1) * D]
                em.op("dve", lambda e, xs=xs, tt=tt: e.tensor_scalar(
                    out=xs, in0=xs, scalar1=st_[:, 24 + 2 * tt:25 + 2 * tt],
                    scalar2=st_[:, 30 + tt:31 + tt],
                    op0=ALU.subtract, op1=ALU.mult),
                    reads=(xkey, SK), writes=(xkey,))

        def tr_part(it, xkey, xb, tkey, dst32):
            blocks = [(tt, hc) for tt in range(NTT) for hc in range(HC)]
            d3 = dst32[:].rearrange("p (h t) -> p h t", h=HC)
            for bi in range(4):
                bank = trb[bi % 2]
                bkey = "tr%d" % (bi % 2)
                tt0, hc0 = blocks[bi * 4]
                for j in range(4):
                    tt, hc = blocks[bi * 4 + j]
                    em.op("pe", lambda e, tt=tt, hc=hc, bank=bank, j=j:
                          e.transpose(out=bank[:, j * P:(j + 1) * P],
                                      in_=xb[:, tt * D + hc * P:
                                             tt * D + (hc + 1) * P],
                                      identity=id_sb[:]),
                          reads=(xkey, "idsb"), writes=(bkey,),
                          track=(j == 3))
                em.op("act", lambda e, bank=bank, tt0=tt0, hc0=hc0:
                      e.copy(out=d3[:, hc0:hc0 + 4, tt0 * P:(tt0 + 1) * P],
                             in_=bank[:]),
                      reads=(bkey,), writes=(tkey,))

        def router(it, s, k):
            cur = it % 2
            th, tl = thi[cur], tlo
            thk = "thi%d" % cur
            em.op("pe", lambda e: e.matmul(
                out=psr[0:NE, 0:TS], lhsT=rbh[:], rhs=ones_r[:],
                start=True, stop=False),
                reads=("rbh", "ones_r"), writes=("tr0",), track=False)
            em.op("pe", lambda e: e.matmul(
                out=psr[0:NE, 0:TS], lhsT=rbl[:], rhs=ones_r[:],
                start=False, stop=False),
                reads=("rbl", "ones_r"), writes=("tr0",), track=False)
            n = 0
            for hc in range(HC):
                for (a, b, ak, bk) in ((wrh, th, "wrh", thk),
                                       (wrh, tl, "wrh", "tlo"),
                                       (wrl, th, "wrl", thk)):
                    n += 1
                    last = (n == 24)
                    em.op("pe", lambda e, a=a, b=b, hc=hc, last=last:
                          e.matmul(out=psr[0:NE, 0:TS],
                                   lhsT=a[:, hc * NE:(hc + 1) * NE],
                                   rhs=b[:, hc * TS:(hc + 1) * TS],
                                   start=False, stop=last),
                          reads=(ak, bk), writes=("tr0",), track=last)
            em.op("act", lambda e: e.copy(out=logt[:], in_=psr[0:NE, 0:TS]),
                  reads=("tr0",), writes=("logt",))
            for tt in range(NTT):
                em.op("pe", lambda e, tt=tt: e.transpose(
                    out=zb[5][:, tt * NE:(tt + 1) * NE],
                    in_=logt[:, tt * P:(tt + 1) * P],
                    identity=id_sb[0:NE, 0:NE]),
                    reads=("logt", "idsb"), writes=("z5",))
            em.op("dve", lambda e: e.tensor_copy(
                out=logits[:, 0:NTT * NE], in_=zb[5][:, 0:NTT * NE]),
                reads=("z5",), writes=("logits",))
            for tt in range(NTT):
                j = (s * TSEQ + k * TS) // P + tt
                lg = logits[:, tt * NE:(tt + 1) * NE]
                c0 = 24 + 2 * tt
                em.op("dve", lambda e, lg=lg, c0=c0: e.tensor_reduce(
                    out=stat[:, c0:c0 + 1], in_=lg, axis=AX.X,
                    op=ALU.max, negate=True),
                    reads=("logits",), writes=("statr",))
                em.op("dve", lambda e, lg=lg, c0=c0, j=j: e.tensor_scalar(
                    out=onehot[:, j * NE:(j + 1) * NE], in0=lg,
                    scalar1=stat[:, c0:c0 + 1], scalar2=0.0,
                    op0=ALU.add, op1=ALU.is_equal),
                    reads=("logits", "statr"), writes=("onehot",))
                em.op("act", lambda e, lg=lg, c0=c0: e.activation(
                    out=sq8[:, 0:NE], in_=lg, func=AF.Sigmoid,
                    bias=stat[:, c0:c0 + 1]),
                    reads=("logits", "statr"), writes=("sq8",))
                em.op("dve", lambda e: e.tensor_scalar(
                    out=sq8[:, NE:2 * NE], in0=sq8[:, 0:NE], scalar1=-1.0,
                    scalar2=1.0, op0=ALU.mult, op1=ALU.add),
                    reads=("sq8",), writes=("sq8",))
                em.op("dve", lambda e: e.reciprocal(
                    out=sq8[:, NE:2 * NE], in_=sq8[:, NE:2 * NE]),
                    reads=("sq8",), writes=("sq8",))
                em.op("dve", lambda e: e.tensor_mul(
                    sq8[:, NE:2 * NE], sq8[:, 0:NE], sq8[:, NE:2 * NE]),
                    reads=("sq8",), writes=("sq8",))
                em.op("dve", lambda e, c0=c0: e.reduce_sum(
                    out=stat[:, c0 + 1:c0 + 2], in_=sq8[:, NE:2 * NE],
                    axis=AX.X), reads=("sq8",), writes=("statr",))
                em.op("dve", lambda e, c0=c0, j=j: e.reciprocal(
                    out=gate[:, j:j + 1], in_=stat[:, c0 + 1:c0 + 2]),
                    reads=("statr",), writes=("gate",))

        def z_part(wsb, wname, rkey, rhs):
            for r in range(24):
                bank = zb[r % 6]
                bkey = "z%d" % (r % 6)
                sl = (r // 6) % 2
                outap = bank[:, sl * TS:(sl + 1) * TS]
                for hc in range(HC):
                    em.op("pe", lambda e, r=r, hc=hc, outap=outap: e.matmul(
                        out=outap,
                        lhsT=wsb[:, hc * K3 + r * P: hc * K3 + (r + 1) * P],
                        rhs=rhs[:, hc * TS:(hc + 1) * TS],
                        start=(hc == 0), stop=(hc == HC - 1)),
                        reads=("%s.%d" % (wname, hc // 2), rkey),
                        writes=(bkey,), track=(hc == HC - 1))
                if r < 8:
                    fg = r
                    em.op("act", lambda e, outap=outap, fg=fg: e.activation(
                        out=SG[:, fg * TS:(fg + 1) * TS], in_=outap,
                        func=AF.Sigmoid), reads=(bkey,), writes=("SGb",))
                    em.op("act", lambda e, outap=outap, fg=fg: e.activation(
                        out=HP[:, fg * TS:(fg + 1) * TS], in_=outap,
                        func=AF.Copy, bias=0.5),
                        reads=(bkey,), writes=("HPb",))
                elif r < 16:
                    fg = r - 8
                    em.op("act", lambda e, outap=outap, fg=fg: e.activation(
                        out=Cb[:, fg * TS:(fg + 1) * TS], in_=outap,
                        func=AF.Sigmoid, scale=-1.0),
                        reads=(bkey,), writes=("Cb",))
                else:
                    fg = r - 16
                    em.op("act", lambda e, outap=outap, fg=fg: e.activation(
                        out=SPb[:, fg * TS:(fg + 1) * TS], in_=outap,
                        func=AF.Sigmoid), reads=(bkey,), writes=("SPb",))
        def post_part(s, k, layer, skey, src32):
            em.op("dve", lambda e: e.tensor_max(HP[:], SG[:], HP[:]),
                  reads=("SGb", "HPb"), writes=("HPb",))
            em.op("dve", lambda e: e.scalar_tensor_tensor(
                out=HP[:], in0=Cb[:], scalar=1.0, in1=HP[:],
                op0=ALU.subtract, op1=ALU.mult),
                reads=("Cb", "HPb"), writes=("HPb",))
            cbase = layer * NSEQ * HC + s * HC
            for fg in range(HC):
                if k == 0:
                    init = 0.0
                    creads = ("Cb", "HPb")
                else:
                    init = carry[:, cbase + fg:cbase + fg + 1]
                    creads = ("Cb", "HPb", "carry")
                em.op("dve", lambda e, fg=fg, init=init: e.tensor_tensor_scan(
                    out=OUT[:, fg * TS:(fg + 1) * TS],
                    data0=Cb[:, fg * TS:(fg + 1) * TS],
                    data1=HP[:, fg * TS:(fg + 1) * TS],
                    initial=init, op0=ALU.mult, op1=ALU.add),
                    reads=creads, writes=("OUTb",))
            em.op("dve", lambda e, cbase=cbase: e.tensor_copy(
                out=carry[:, cbase:cbase + HC],
                in_=OUT[:].rearrange("p (h t) -> p h t", h=HC)[:, :,
                                                             TS - 1:TS]),
                reads=("OUTb",), writes=("carry",))
            em.op("dve", lambda e: e.tensor_add(OUT[:], OUT[:], src32[:]),
                  reads=("OUTb", skey), writes=("OUTb",))
            em.op("dve", lambda e: e.tensor_mul(OUT[:], SPb[:], OUT[:]),
                  reads=("SPb", "OUTb"), writes=("OUTb",))
            em.op("dve", lambda e: e.tensor_sub(CMB[:], src32[:], OUT[:]),
                  reads=(skey, "OUTb"), writes=("CMBb",))

        if "a1" in phases:
            NIT = len(iters)

            def a1_load(it):
                s2, k2 = iters[it]
                em.dma("sp", "in0%s" % "ab"[it % 2],
                       lambda e, s2=s2, k2=k2, it=it: e.dma_start(
                           out=in0[it % 2][:].rearrange(
                               "p (a h) -> p a h", a=NTT),
                           in_=tokslice(x_h, s2 * TSEQ + k2 * TS)),
                       writes=("in0%d" % (it % 2),))

            def a1_cast(it):
                cur = it % 2
                st = iters[it][0] * NST_SEQ + iters[it][1]
                em.op("act", lambda e, cur=cur: e.copy(
                    out=thi[cur][:], in_=t32[cur][:]),
                    reads=("t32%d" % cur,), writes=("thi%d" % cur,))
                em.op("dve", lambda e, cur=cur: e.tensor_sub(
                    tlo[:], t32[cur][:], thi[cur][:]),
                    reads=("t32%d" % cur, "thi%d" % cur), writes=("tlo",))
                em.dma("pool", "sthi%s" % "ab"[cur],
                       lambda e, cur=cur, st=st: e.dma_start(
                           out=xnthi_h[st], in_=thi[cur][:]),
                       reads=("thi%d" % cur,), writes=("hbm.xnthi.%d" % st,))

            for it in range(-2, NIT):
                if 0 <= it + 2 < NIT:
                    a1_load(it + 2)
                    ln_part(it + 2, "in0%d" % ((it + 2) % 2), in0[(it + 2) % 2])
                if 0 <= it + 1 < NIT:
                    c1 = (it + 1) % 2
                    tr_part(it + 1, "in0%d" % c1, in0[c1], "t32%d" % c1,
                            t32[c1])
                    a1_cast(it + 1)
                if 0 <= it:
                    cur = it % 2
                    z_part(w0_sb, "w0", "thi%d" % cur, thi[cur])
                    s1, k1 = iters[it]
                    st = s1 * NST_SEQ + k1
                    post_part(s1, k1, 0, "t32%d" % (it % 2), t32[it % 2])
                    em.dma("pool", "scmb", lambda e, st=st: e.dma_start(
                        out=h1t_h[st], in_=CMB[:]),
                        reads=("CMBb",), writes=("hbm.h1t.%d" % st,))
                if 0 <= it + 1 < NIT:
                    s2, k2 = iters[it + 1]
                    router(it + 1, s2, k2)

        if "a2" in phases:
            NIT = len(iters)

            rot = [in0[0], in0[1], t32[1]]
            rkeys = ["in00", "in01", "t321"]

            def a2_load(it):
                s2, k2 = iters[it]
                st2 = s2 * NST_SEQ + k2
                em.dma("sp", "in0%s" % "abc"[it % 3],
                       lambda e, st2=st2, it=it: e.dma_start(
                           out=rot[it % 3][:], in_=h1t_h[st2]),
                       reads=("hbm.h1t.%d" % st2,),
                       writes=(rkeys[it % 3],))

            def a2_trback(it):
                s1, k1 = iters[it]
                stok, stokkey = t32[0], "t320"
                s3 = stok[:].rearrange("p (a h) -> p a h", a=NTT)
                blocks = [(tt, hc) for tt in range(NTT) for hc in range(HC)]
                for bi in range(4):
                    bank = trb[bi % 2]
                    bkey = "tr%d" % (bi % 2)
                    tt0, hc0 = blocks[bi * 4]
                    for j in range(4):
                        tt, hc = blocks[bi * 4 + j]
                        em.op("pe", lambda e, tt=tt, hc=hc, bank=bank, j=j:
                              e.transpose(
                                  out=bank[:, j * P:(j + 1) * P],
                                  in_=CMB[:, hc * TS + tt * P:
                                          hc * TS + (tt + 1) * P],
                                  identity=id_sb[:]),
                              reads=("CMBb", "idsb"), writes=(bkey,),
                              track=(j == 3))
                    em.op("act", lambda e, bank=bank, tt0=tt0, hc0=hc0:
                          e.copy(out=s3[:, tt0, hc0 * P:(hc0 + 4) * P],
                                 in_=bank[:]),
                          reads=(bkey,), writes=(stokkey,))
                em.dma("pool", "sst", lambda e, s1=s1, k1=k1: e.dma_start(
                    out=tokslice(s_h, s1 * TSEQ + k1 * TS), in_=s3),
                    reads=(stokkey,),
                    writes=("hbm.s.%d" % (s1 * TSEQ + k1 * TS),))

            def a2_cast(it):
                em.op("dve", lambda e, it=it: e.tensor_copy(
                    out=thi[it % 2][:], in_=rot[it % 3][:]),
                    reads=(rkeys[it % 3],), writes=("thi%d" % (it % 2),))

            for it in range(0, NIT + 1):
                if it < NIT:
                    if it == 0:
                        a2_load(0)
                        a2_load(1)
                        a2_cast(0)
                    if it + 2 < NIT:
                        a2_load(it + 2)
                    if it + 1 < NIT:
                        a2_cast(it + 1)
                    cur = it % 2
                    z_part(w1_sb, "w1", "thi%d" % cur, thi[cur])
                if it - 1 >= 0:
                    a2_trback(it - 1)
                if it < NIT:
                    s1, k1 = iters[it]
                    post_part(s1, k1, 1, rkeys[it % 3], rot[it % 3])

        if "b" in phases:
            es_a.close()
            es_w0.close()
            es_w1.close()
            em.set_floor()
            exp = [sbuf("exp%d" % e2, [P, HC * D], BF16) for e2 in range(NE)]
            bx = [sbuf("bx%d" % i, [P, NTT * D]) for i in range(2)]
            bs = [sbuf("bs%d" % i, [P, NTT * D]) for i in range(2)]
            bh = [sbuf("bh%d" % i, [P, NTT * D], BF16) for i in range(2)]
            bo = [sbuf("bo%d" % i, [P, NTT * D]) for i in range(2)]
            bacc = [sbuf("bacc%d" % i, [P, D]) for i in range(2)]
            bg = sbuf("bg", [P, D])
            for ep0 in range(NE // 2):
                for half in range(2):
                    for e2 in (2 * ep0, 2 * ep0 + 1):
                        em.dma("pool", "exp%d.%d" % (e2, half),
                               lambda e, e2=e2, half=half: e.dma_start(
                                   out=exp[e2][:, half * 4 * D:
                                               (half + 1) * 4 * D],
                                   in_=ew_h[e2 * D + half * 4 * P:
                                            e2 * D + (half + 1) * 4 * P,
                                            :].rearrange(
                                       "(c p) d -> p c d", p=P)),
                               writes=("exp%d.%d" % (e2, half),))

            def bload(bst, slot):
                tok0 = bst * TS
                s = tok0 // TSEQ
                k = (tok0 % TSEQ) // TS
                stA = s * NST_SEQ + k
                em.dma("sp", "bx%s" % "ab"[slot],
                       lambda e, tok0=tok0, slot=slot: e.dma_start(
                           out=bx[slot][:].rearrange(
                               "p (a h) -> p a h", a=NTT),
                           in_=tokslice(x_h, tok0)),
                       writes=("bx%d" % slot,))
                em.dma("sp", "bs%s" % "ab"[slot],
                       lambda e, tok0=tok0, slot=slot: e.dma_start(
                           out=bs[slot][:].rearrange(
                               "p (a h) -> p a h", a=NTT),
                           in_=tokslice(s_h, tok0)),
                       reads=("hbm.s.%d" % tok0,),
                       writes=("bs%d" % slot,))
                em.dma("sp", "bh%s" % "ab"[slot],
                       lambda e, stA=stA, slot=slot: e.dma_start(
                           out=bh[slot][:], in_=xnthi_h[stA]),
                       reads=("hbm.xnthi.%d" % stA,),
                       writes=("bh%d" % slot,))

            bload(0, 0)
            pairbanks = [[0, 1, 2, 3], [4, 5, 6, 7]]
            allb = zb + trb
            bkeys = ["z%d" % i for i in range(6)] + ["tr0", "tr1"]

            def bias_part(j):
                bslot = j % 2
                bk = "bacc%d" % bslot
                em.op("pe", lambda e, j=j: e.transpose(
                    out=trb[0][0:NE, 0:P],
                    in_=onehot[:, j * NE:(j + 1) * NE],
                    identity=id_sb[:]),
                    reads=("onehot", "idsb"), writes=("tr0",))
                em.op("act", lambda e: e.copy(
                    out=ohT[:], in_=trb[0][0:NE, 0:P]),
                    reads=("tr0",), writes=("ohT",))
                for g in range(2):
                    em.op("pe", lambda e, g=g: e.matmul(
                        out=trb[g][:], lhsT=ohT[:],
                        rhs=ebh_sb[:, g * 512:(g + 1) * 512],
                        start=True, stop=True),
                        reads=("ohT", "ebh"), writes=("tr%d" % g,))
                for g in range(2):
                    em.op("dve", lambda e, g=g, bslot=bslot: e.tensor_copy(
                        out=bacc[bslot][:, g * 512:(g + 1) * 512],
                        in_=trb[g][:]),
                        reads=("tr%d" % g,), writes=(bk,))

            bias_part(0)
            for bst in range(NST):
                cur = bst % 2
                if bst + 1 < NST:
                    bload(bst + 1, (bst + 1) % 2)
                hkey = "bh%d" % cur
                for tt in range(NTT):
                    j = bst * NTT + tt
                    cj = j % 2
                    bk = "bacc%d" % cj
                    for ep in range(NE // 2):
                        grp = pairbanks[ep % 2]
                        if ep == 3 and j + 1 < NST * NTT:
                            bias_part(j + 1)
                        for ei in range(2):
                            e2 = ep * 2 + ei
                            for g in range(2):
                                bi = grp[ei * 2 + g]
                                for hc in range(HC):
                                    em.op("pe", lambda e, e2=e2, hc=hc, g=g,
                                          bank=allb[bi], tt=tt, cur=cur:
                                          e.matmul(
                                              out=bank[:, 0:512],
                                              lhsT=bh[cur][
                                                  :, hc * TS + tt * P:
                                                  hc * TS + (tt + 1) * P],
                                              rhs=exp[e2][
                                                  :, hc * D + g * 512:
                                                  hc * D + (g + 1) * 512],
                                              start=(hc == 0),
                                              stop=(hc == HC - 1)),
                                          reads=(hkey,
                                                 "exp%d.%d" % (e2, hc // 4)),
                                          writes=(bkeys[bi],),
                                          track=(hc == HC - 1))
                        for ei in range(2):
                            e2 = ep * 2 + ei
                            oh_col = onehot[:, j * NE + e2:j * NE + e2 + 1]
                            for g in range(2):
                                bi = grp[ei * 2 + g]
                                em.op("dve", lambda e, bank=allb[bi], g=g,
                                      oh_col=oh_col, cj=cj:
                                      e.scalar_tensor_tensor(
                                          out=bacc[cj][:,
                                                       g * 512:(g + 1) * 512],
                                          in0=bank[:, 0:512],
                                          scalar=oh_col,
                                          in1=bacc[cj][:,
                                                       g * 512:(g + 1) * 512],
                                          op0=ALU.mult, op1=ALU.add),
                                      reads=(bkeys[bi], "onehot", bk),
                                      writes=(bk,))
                    em.op("dve", lambda e, j=j, cj=cj: e.tensor_scalar(
                        out=bacc[cj][:], in0=bacc[cj][:],
                        scalar1=gate[:, j:j + 1],
                        scalar2=None, op0=ALU.mult),
                        reads=(bk, "gate"), writes=(bk,))
                    em.op("act", lambda e, cj=cj: e.activation(
                        out=bg[:], in_=bacc[cj][:], func=AF.Silu),
                        reads=(bk,), writes=("bg",))
                    em.op("dve", lambda e, tt=tt, cur=cur: e.tensor_mul(
                        bg[:], bs[cur][:, tt * D:(tt + 1) * D], bg[:]),
                        reads=("bs%d" % cur, "bg"), writes=("bg",))
                    em.op("pool", lambda e, tt=tt, cur=cur: e.tensor_add(
                        bo[cur][:, tt * D:(tt + 1) * D],
                        bx[cur][:, tt * D:(tt + 1) * D], bg[:]),
                        reads=("bx%d" % cur, "bg"), writes=("bo%d" % cur,))
                em.dma("pool", "bo%s" % "ab"[cur],
                       lambda e, bst=bst, cur=cur: e.dma_start(
                           out=tokslice(out_h, bst * TS),
                           in_=bo[cur][:].rearrange(
                               "p (a h) -> p a h", a=NTT)),
                       reads=("bo%d" % cur,))

        em.finalize_sp()

        block = es.enter_context(nc.Block())

        @block.sync
        def _(e):
            for f in em.streams["sp"]:
                f(e)

        @block.tensor
        def _(e):
            for f in em.streams["pe"]:
                f(e)

        @block.scalar
        def _(e):
            for f in em.streams["act"]:
                f(e)

        @block.vector
        def _(e):
            for f in em.streams["dve"]:
                f(e)

        @block.gpsimd
        def _(e):
            for f in em.streams["pool"]:
                f(e)

    nc._em_oplog = em.oplog
    return nc


def kernel(L0_attn_h, L0_attn_W0, L0_attn_W1, L0_ffn_router_w,
           L0_ffn_router_b, L0_ffn_expert_w, L0_ffn_expert_b):
    x = np.ascontiguousarray(np.asarray(L0_attn_h, np.float32)
                             ).reshape(16 * TSEQ, D)
    w0 = np.ascontiguousarray(np.asarray(L0_attn_W0, np.float32))
    w1 = np.ascontiguousarray(np.asarray(L0_attn_W1, np.float32))
    rw = np.ascontiguousarray(
        np.asarray(L0_ffn_router_w, np.float32).reshape(HC, P, NE)
        .transpose(1, 0, 2).reshape(P, HC * NE))
    rb = np.ascontiguousarray(np.asarray(L0_ffn_router_b, np.float32)
                              ).reshape(1, NE)
    ew = np.ascontiguousarray(np.asarray(L0_ffn_expert_w, np.float32)
                              ).reshape(NE * D, D)
    eb = np.ascontiguousarray(np.asarray(L0_ffn_expert_b, np.float32))
    ident = np.eye(P, dtype=np.float32)

    nc = build_nc()
    in_maps = []
    for c in range(N_CORES):
        in_maps.append(dict(x=x[c * TOK:(c + 1) * TOK], w0=w0, w1=w1,
                            rw=rw, rb=rb, ew=ew, eb=eb, ident=ident))
    res = run_bass_kernel_spmd(nc, in_maps, core_ids=list(range(N_CORES)))
    out = np.concatenate([res.results[c]["out"] for c in range(N_CORES)],
                         axis=0)
    return out.reshape(16, TSEQ, D)
